# version 12
# baseline (speedup 1.0000x reference)
import contextlib
import numpy as np
import ml_dtypes
import concourse.bass as bass
import concourse.mybir as mybir
from concourse.bass_utils import run_bass_kernel_spmd

F32 = mybir.dt.float32
BF16 = mybir.dt.bfloat16
AF = mybir.ActivationFunctionType
ALU = mybir.AluOpType
AX = mybir.AxisListType

T = 2048
D = 1024
NT = T // 128
ND = D // 128
EPS_LN = 1e-5
EPS_RMS = 1e-6
NEG_BIG = -1.0e9

ENGINES = ("tensor", "vector", "scalar", "gpsimd", "sync")

HEAP_BYTES = 194 * 1024
SMALL_F32 = 3072


class Prog:
    def __init__(self, env, ename, eng):
        self.env = env
        self.e = ename
        self.eng = eng
        self.c = {k: 0 for k in env["sems"]}

    def op(self, ename, f, sem=None, inc=1):
        tgt = None
        if sem is not None:
            self.c[sem] += inc
            tgt = self.c[sem]
        if self.e == ename:
            ins = f(self.eng)
            if sem is not None:
                ins.then_inc(self.env["sems"][sem], inc)
        return tgt

    def wait(self, ename, sem, val):
        if self.e == ename and val is not None and val > 0:
            self.eng.wait_ge(self.env["sems"][sem], val)

    def val(self, sem):
        return self.c[sem]


def carve(heap, off_bytes, dtype, shape):
    assert off_bytes % 4 == 0
    n = 1
    for s in shape:
        n *= s
    if dtype == F32:
        v = heap[:, off_bytes // 4: off_bytes // 4 + n]
    else:
        assert n % 2 == 0
        v = heap[:, off_bytes // 4: off_bytes // 4 + n // 2].bitcast(BF16)
    if len(shape) == 1:
        return v
    pat = "p (" + " ".join(f"a{i}" for i in range(len(shape))) + ") -> p " + \
          " ".join(f"a{i}" for i in range(len(shape)))
    kw = {f"a{i}": s for i, s in enumerate(shape)}
    return v.rearrange(pat, **kw)


def emit_consts(p):
    env = p.env
    sm = env["small"]
    ident = env["ident"]
    mask = env["mask"]
    ones_r = env["ones_row"]
    ones_rb = env["ones_row_bf"]
    gv = p.op("gpsimd", lambda g: g.iota(
        mask, pattern=[[1, 128]], base=0, channel_multiplier=-1,
        allow_small_or_imprecise_dtypes=True), sem="sG")
    p.wait("gpsimd", "sG", gv)
    gv = p.op("gpsimd", lambda g: g.tensor_single_scalar(ident, mask, 0.0, ALU.is_equal), sem="sG")
    p.wait("gpsimd", "sG", gv)
    gv = p.op("gpsimd", lambda g: g.tensor_single_scalar(mask, mask, 0.0, ALU.is_gt), sem="sG")
    p.wait("gpsimd", "sG", gv)
    p.op("gpsimd", lambda g: g.tensor_scalar_mul(mask, mask, NEG_BIG), sem="sG")
    p.op("gpsimd", lambda g: g.memset(ones_r, 1.0), sem="sG")
    p.op("gpsimd", lambda g: g.memset(ones_rb, 1.0), sem="sG")
    p.op("gpsimd", lambda g: g.memset(env["ones_col_bf"], 1.0), sem="sG")
    scr_a = env["stats"][:, 0:16]
    scr_b = env["stats"][:, 16:32]
    gv = p.op("gpsimd", lambda g: g.iota(
        scr_a, pattern=[[1, 16]], base=0, channel_multiplier=0,
        allow_small_or_imprecise_dtypes=True), sem="sG")
    p.wait("gpsimd", "sG", gv)
    gv = p.op("gpsimd", lambda g: g.iota(
        scr_b, pattern=[[0, 16]], base=0, channel_multiplier=1,
        allow_small_or_imprecise_dtypes=True), sem="sG")
    p.wait("gpsimd", "sG", gv)
    gv = p.op("gpsimd", lambda g: g.tensor_single_scalar(
        scr_b, scr_b, 63.5, ALU.is_gt), sem="sG")
    p.wait("gpsimd", "sG", gv)
    for c in range(2):
        gv = p.op("gpsimd", lambda g, c=c: g.tensor_tensor(
            env["stats"][:, 32:48], scr_a, scr_b, ALU.subtract), sem="sG")
        p.wait("gpsimd", "sG", gv)
        gv = p.op("gpsimd", lambda g, c=c: g.tensor_single_scalar(
            env["sel64"][c], env["stats"][:, 32:48], 2.0 * c, ALU.is_equal), sem="sG")
        p.wait("gpsimd", "sG", gv)
    for m0 in range(4):
        gv = p.op("gpsimd", lambda g, m0=m0: g.tensor_single_scalar(
            env["self4"][m0], scr_a, float(m0), ALU.is_equal), sem="sG")
        p.wait("gpsimd", "sG", gv)
    gv = p.op("gpsimd", lambda g: g.tensor_single_scalar(
        env["sall4"], scr_a, 3.5, ALU.is_lt), sem="sG")
    p.wait("gpsimd", "sG", gv)


def emit_bias_pp(p, rows_sb, nchunk, out_pp, wait=None, scale=None, psum=None):
    env = p.env
    ps = psum if psum is not None else env["psB0"]
    if wait is not None:
        p.wait("tensor", *wait)
    ident = env["ident"]
    p.wait("tensor", "sG", env["consts_done"])
    p.wait("tensor", "sV", p.val("sV"))
    p.op("tensor", lambda t: t.transpose(ps[:, 0:nchunk], rows_sb, ident[0:nchunk, 0:nchunk]), sem="sT")
    tv = p.val("sT")
    p.wait("vector", "sT", tv)
    if scale is None:
        p.op("vector", lambda v: v.tensor_copy(out_pp, ps[:, 0:nchunk]), sem="sV")
    else:
        p.op("vector", lambda v: v.tensor_scalar_mul(out_pp, ps[:, 0:nchunk], scale), sem="sV")
    return p.val("sV")


def emit_bcast(p, row_sb, n, out_bc, wait=None):
    env = p.env
    ones_r = env["ones_row"]
    if wait is not None:
        p.wait("tensor", *wait)
    p.wait("tensor", "sG", env["consts_done"])
    p.wait("tensor", "sV", p.val("sV"))
    done = []
    for o in range(0, n, 512):
        w = min(512, n - o)
        ps = env["psB1"] if (o // 512) % 2 else env["psB0"]
        p.op("tensor", lambda t, o=o, w=w, ps=ps: t.matmul(
            ps[:, 0:w], ones_r, row_sb[:, o:o + w], start=True, stop=True), sem="sT")
        tv = p.val("sT")
        p.wait("vector", "sT", tv)
        p.op("vector", lambda v, o=o, w=w, ps=ps: v.tensor_copy(out_bc[:, o:o + w], ps[:, 0:w]), sem="sV")
        done.append(p.val("sV"))
    return p.val("sV")


class LNStat:
    def __init__(self, env):
        self.ap = env["stats"]
        self.n = 0

    def col(self):
        c = self.n
        self.n += 2
        assert self.n <= 256
        return self.ap[:, c:c + 1]

    def misc(self, idx):
        assert 0 <= idx < 32
        return self.ap[:, 224 + idx:225 + idx]


def emit_ln_tile(p, x_ap, out_ap, scr_ap, stats, eps, gbc=None, out2_ap=None):
    s1 = stats.col()
    s2 = stats.col()
    nm = stats.col()
    var = stats.col()
    var2 = stats.col()
    sd = stats.col()
    rst = stats.col()
    sv1 = None
    if gbc is None:
        sv1 = p.op("vector", lambda v: v.reduce_sum(s1, x_ap, axis=AX.X), sem="sV")
    sqv = p.op("scalar", lambda s: s.activation(
        scr_ap, x_ap, AF.Square, bias=0.0, scale=1.0, accum_out=s2), sem="sS")
    p.wait("vector", "sS", sqv)
    if gbc is None:
        p.wait("vector", "sV", sv1)
        nmv = p.op("vector", lambda v: v.tensor_scalar_mul(nm, s1, -1.0 / D), sem="sV")
        p.wait("vector", "sV", nmv)
        varv = p.op("vector", lambda v: v.tensor_tensor(var, nm, nm, ALU.mult), sem="sV")
        p.wait("vector", "sV", varv)
        v2 = p.op("vector", lambda v: v.scalar_tensor_tensor(
            var2, s2, 1.0 / D, var, ALU.mult, ALU.subtract), sem="sV")
        p.wait("vector", "sV", v2)
        p.op("vector", lambda v: v.tensor_scalar_add(var2, var2, float(eps)), sem="sV")
    else:
        p.op("vector", lambda v: v.tensor_scalar(
            var2, s2, 1.0 / D, float(eps), ALU.mult, ALU.add), sem="sV")
    vv = p.val("sV")
    p.wait("scalar", "sV", vv)
    p.op("scalar", lambda s: s.activation(sd, var2, AF.Sqrt, bias=0.0, scale=1.0), sem="sS")
    sv = p.val("sS")
    p.wait("vector", "sS", sv)
    rcv = p.op("vector", lambda v: v.reciprocal(rst, sd), sem="sV")
    p.wait("vector", "sV", rcv)
    if gbc is None:
        p.op("vector", lambda v: v.tensor_scalar(
            out_ap, x_ap, nm, rst, ALU.add, ALU.mult), sem="sV")
        if out2_ap is not None:
            p.op("vector", lambda v: v.tensor_scalar(
                out2_ap, x_ap, nm, rst, ALU.add, ALU.mult), sem="sV")
    else:
        p.op("vector", lambda v: v.scalar_tensor_tensor(
            out_ap, x_ap, rst, gbc, ALU.mult, ALU.mult), sem="sV")
        if out2_ap is not None:
            p.op("vector", lambda v: v.scalar_tensor_tensor(
                out2_ap, x_ap, rst, gbc, ALU.mult, ALU.mult), sem="sV")
    return p.val("sV")




def emit_bound_prep(p, env, qt, kt, sels_q, sels_k, PF, BROW, BT, NB, st,
                    qk_ready, barrier, nheads=4):
    psA = env["psA"]
    kmc = st.col()
    km16 = kmc[0:16, :]
    p.wait("scalar", "sS", barrier[0])
    p.wait("scalar", "sT", barrier[2])
    p.wait("scalar", "sV", qk_ready)
    p.wait("vector", "sT", barrier[2])
    p.wait("gpsimd", "sT", barrier[2])
    p.wait("tensor", "sS", barrier[0])
    bz = p.op("gpsimd", lambda g: g.memset(BROW[0:16, 0:T], 0.0), sem="sG")
    last_mm = [0]

    def side(src_t, psrow, sels):
        nch = len(sels)
        for c in range(nch):
            p.wait("scalar", "sT", last_mm[0])
            sqd = p.op("scalar", lambda s, c=c, src_t=src_t: s.activation(
                PF[:, 0:T], src_t[:, c, :], AF.Square, bias=0.0, scale=1.0), sem="sS")
            p.wait("tensor", "sS", sqd)
            for sl in range(4):
                p.op("tensor", lambda t, sl=sl, c=c: t.matmul(
                    psrow[0:16, sl * 512:(sl + 1) * 512],
                    sels[c], PF[:, sl * 512:(sl + 1) * 512],
                    start=(c == 0), stop=(c == nch - 1)),
                    sem="sT" if sl == 3 and c == nch - 1 else None)
            last_mm[0] = p.val("sT")
        return last_mm[0]

    kv = side(kt, psA, sels_k)
    p.wait("vector", "sT", kv)
    kmv = p.op("vector", lambda v: v.reduce_max(
        km16, psA[0:16, 0:T], axis=AX.X), sem="sV")
    p.wait("tensor", "sV", kmv)
    qv = side(qt, psA, sels_q)
    p.wait("scalar", "sT", qv)
    p.wait("scalar", "sV", kmv)
    p.wait("scalar", "sG", bz)
    sqv = p.op("scalar", lambda s: s.activation(
        BROW[0:16, 0:T], psA[0:16, 0:T], AF.Sqrt, bias=0.0, scale=km16), sem="sS")
    p.wait("sync", "sS", sqv)
    p.op("sync", lambda s: s.dma_start(out=BT, in_=BROW[0:16, 0:T],
                                       transpose=True), sem="dT", inc=16)
    btv = p.val("dT")
    p.wait("vector", "dT", btv)
    p.op("vector", lambda v: v.tensor_scalar_mul(
        NB, BT[:, :, 0:nheads], -1.0), sem="sV")
    return p.val("sV"), p.val("sS"), p.val("sT")


def emit_attn_core(p, cfg):
    env = p.env
    NPd = cfg["np"]
    P_ = cfg["P"]
    st = cfg["astat"]
    psA, psA2 = env["psA"], env["psA2"]
    dn0 = [st.col() for _ in range(NPd)]
    dn1 = [st.col() for _ in range(NPd)]
    dn2 = [st.col() for _ in range(NPd)]
    den = [st.col() for _ in range(NPd)]
    rcc = [st.col() for _ in range(NPd)]
    exp_done = [0, 0]
    avevac_done = [0, 0]
    avbank_free = [0, 0]
    trPA = [0] * NPd
    trPX = [0] * NPd
    avmm_done = [0] * NPd
    den_done = [0] * NPd
    den_used = [0] * NPd
    rcp_done = [0] * NPd
    pq = []

    def flush(pend):
        itp, ibp, ipp, gate = pend
        tb = psA2 if ibp == 0 else psA
        p.wait("tensor", "sS", gate)
        p.wait("tensor", "sV", avbank_free[(ibp + 1) % 2])
        mv = cfg["av"](tb, itp, ipp, trPA[ipp], trPX[ipp])
        avmm_done[ipp] = mv
        p.wait("vector", "sT", mv)
        p.wait("vector", "sV", rcp_done[ipp])
        ev = cfg["evac"](tb, itp, rcc[ipp])
        avevac_done[(ibp + 1) % 2] = ev
        avbank_free[(ibp + 1) % 2] = ev
        den_used[ipp] = ev

    for i, it in enumerate(cfg["iters"]):
        ib = i % 2
        ip = i % NPd
        buf = psA if ib == 0 else psA2
        nw = it["nw"]
        w1 = min(nw, 1024) - 512 if nw > 512 else 0
        wA = nw - 1024 if nw > 1024 else 0
        p.wait("tensor", "sS", exp_done[ib])
        p.wait("tensor", "sV", avevac_done[ib])
        if i < 2 and cfg.get("ready"):
            p.wait("tensor", "sV", cfg["ready"])
        if i < 2 and cfg.get("ready_s"):
            p.wait("tensor", "sS", cfg["ready_s"])
        sc1 = cfg["scores"](buf, it, 512, w1) if w1 else None
        scA = cfg["scores"](buf, it, 1024, wA) if wA else None
        scX = cfg["scores"](buf, it, 0, min(nw, 512))
        if len(pq) == 2:
            flush(pq.pop(0))
        pq.append([it, ib, ip, 0])
        mk = {}
        if cfg.get("mask"):
            part = it["diag_part"]
            p.wait("vector", "sT", {0: scX, 1: sc1, 2: scA}[part])
            mk[part] = cfg["mask"](buf, it)
        nb = cfg["bias"](it)
        p.wait("scalar", "sV", rcp_done[ip])
        p.wait("scalar", "sG", den_done[ip])
        p.wait("scalar", "dT", trPX[ip])
        eB1 = None
        if w1:
            p.wait("scalar", "sT", sc1)
            if 1 in mk:
                p.wait("scalar", "sV", mk[1])
            eB1 = p.op("scalar", lambda s: s.activation(
                P_[:, ip, 512:512 + w1], buf[:, 512:512 + w1], AF.Exp, bias=nb,
                scale=1.0, accum_out=dn1[ip]), sem="sS")
        eA = None
        if wA:
            p.wait("scalar", "sT", scA)
            if 2 in mk:
                p.wait("scalar", "sV", mk[2])
            eA = p.op("scalar", lambda s: s.activation(
                P_[:, ip, 1024:nw], buf[:, 1024:nw], AF.Exp, bias=nb,
                scale=1.0, accum_out=dn2[ip]), sem="sS")
        if w1:
            p.wait("sync", "sS", eA if eA is not None else eB1)
            p.wait("sync", "sT", avmm_done[ip])
            p.op("sync", lambda s: s.dma_start(
                out=cfg["PT"][:, ip, 4:nw // 128, :], in_=P_[:, ip, 512:nw],
                transpose=True), sem="dT", inc=16)
            trPA[ip] = p.val("dT")
        p.wait("scalar", "sT", scX)
        if 0 in mk:
            p.wait("scalar", "sV", mk[0])
        eX = p.op("scalar", lambda s: s.activation(
            P_[:, ip, 0:min(nw, 512)], buf[:, 0:min(nw, 512)], AF.Exp,
            bias=nb, scale=1.0, accum_out=dn0[ip]), sem="sS")
        p.wait("sync", "sS", eX)
        if not w1:
            p.wait("sync", "sT", avmm_done[ip])
        p.op("sync", lambda s: s.dma_start(
            out=cfg["PT"][:, ip, 0:min(nw, 512) // 128, :], in_=P_[:, ip, 0:min(nw, 512)],
            transpose=True), sem="dT", inc=16)
        trPX[ip] = p.val("dT")
        if not w1:
            trPA[ip] = trPX[ip]
        exp_done[ib] = eX
        pq[-1][3] = eB1 if eB1 is not None else eX
        p.wait("gpsimd", "sS", eX)
        p.wait("gpsimd", "sV", den_used[ip])
        dv = p.op("gpsimd", lambda g: (
            g.tensor_tensor(den[ip], dn0[ip], dn1[ip], ALU.add) if w1 else
            g.tensor_copy(den[ip], dn0[ip])), sem="sG")
        if wA:
            p.wait("gpsimd", "sG", dv)
            dv = p.op("gpsimd", lambda g: g.tensor_tensor(
                den[ip], den[ip], dn2[ip], ALU.add), sem="sG")
        den_done[ip] = dv
        p.wait("vector", "sG", dv)
        rcp_done[ip] = p.op("vector", lambda v: v.reciprocal(rcc[ip], den[ip]), sem="sV")
    for pend in pq:
        flush(pend)


def ceil_div(a, b):
    return (a + b - 1) // b


class StopEmission(Exception):
    pass


def dump(p, name, src_ap, waits=()):
    env = p.env
    d = env.get("dbg") or {}
    if name in d:
        for w in waits:
            p.wait("scalar", *w)
        p.op("scalar", lambda s: s.dma_start(out=d[name], in_=src_ap), sem="dS", inc=16)
    if env.get("stop") == name:
        raise StopEmission()


def emit_l0_attn(p, w, xin, xout, wpre):
    env = p.env
    heap = env["heap"]
    small = env["small"]
    FM = carve(heap, 0, BF16, [ND, T])
    QTG = carve(heap, 32768, BF16, [2, T])
    KTG = carve(heap, 40960, BF16, [2, T])
    VG = carve(heap, 49152, BF16, [NT, 256])
    LAT = carve(heap, 57344, BF16, [2, T])
    O = carve(heap, 65536, BF16, [NT, D])
    P = carve(heap, 98304, BF16, [3, T])
    PT = carve(heap, 110592, BF16, [3, NT, 128])
    XTa = carve(heap, 122880, F32, [6, D])
    LNB = carve(heap, 147456, BF16, [2, D])
    WQ = carve(heap, 151552, BF16, [ND, D])
    WL = carve(heap, 167936, BF16, [ND, 256])
    WK = carve(heap, 172032, BF16, [2, D])
    WV = carve(heap, 176128, BF16, [2, D])
    WO = carve(heap, 180224, BF16, [ND, D])

    BC1 = small[:, 0:1024]
    BC2 = small[:, 1024:2048]
    BPP = small[:, 2048:2080]
    stats = LNStat(env)

    psA = env["psA"]
    psA2 = env["psA2"]
    psB0 = env["psB0"]
    psB1 = env["psB1"]
    psC0 = env["psC0"]
    psC1 = env["psC1"]

    SCALE = 0.125

    rows_bq = XTa[0:8, 0, 0:128]
    rows_bl = XTa[0:2, 1, 0:128]
    rows_bk = XTa[0:8, 2, 0:128]
    row_bv = XTa[0:1, 3, :]
    row_bo = XTa[0:1, 4, :]
    p.op("scalar", lambda g: g.dma_start(
        out=rows_bq, in_=w["bq"].rearrange("(c p) -> c p", p=128)), sem="dX", inc=16)
    p.op("scalar", lambda g: g.dma_start(
        out=rows_bl, in_=w["bl"].rearrange("(c p) -> c p", p=128)), sem="dX", inc=16)
    p.op("scalar", lambda g: g.dma_start(
        out=rows_bk, in_=w["bk"].rearrange("(c p) -> c p", p=128)), sem="dX", inc=16)
    p.op("scalar", lambda g: g.dma_start(out=row_bv, in_=w["bv"].unsqueeze(0)), sem="dX", inc=16)
    p.op("scalar", lambda g: g.dma_start(out=row_bo, in_=w["bo"].unsqueeze(0)), sem="dX", inc=16)
    wload = p.val("dX")

    emit_bias_pp(p, rows_bq, 8, BPP[:, 0:8], wait=("dX", wload), psum=psB0)
    emit_bias_pp(p, rows_bl, 2, BPP[:, 8:10], psum=psB1)
    emit_bias_pp(p, rows_bk, 8, BPP[:, 10:18], psum=psC0)
    emit_bcast(p, row_bv, D, BC1)
    emit_bcast(p, row_bo, D, BC2)
    bias_done = p.val("sV")

    ln_done = [0] * NT
    tr_done = [0] * NT
    slot_free = [0, 0]
    lnb_free = [0, 0]
    for it in range(NT):
        ds = XTa[:, it % 2, :]
        scr = XTa[:, 2 + it % 2, :]
        if it < 2:
            p.wait("scalar", "sV", bias_done)
        p.wait("scalar", "sV", slot_free[it % 2])
        p.op("scalar", lambda g, it=it, ds=ds: g.dma_start(
            out=ds, in_=xin[it * 128:(it + 1) * 128, :]), sem="dX", inc=16)
        xv = p.val("dX")
        p.wait("vector", "dX", xv)
        p.wait("scalar", "dX", xv)
        p.wait("vector", "dT", lnb_free[it % 2])
        ln_done[it] = emit_ln_tile(p, ds, LNB[:, it % 2, :], scr, stats, EPS_LN)
        slot_free[it % 2] = ln_done[it]
        p.wait("sync", "sV", ln_done[it])
        p.op("sync", lambda s, it=it: s.dma_start(
            out=FM[:, :, it * 128:(it + 1) * 128], in_=LNB[:, it % 2, :],
            transpose=True), sem="dT", inc=16)
        tr_done[it] = p.val("dT")
        lnb_free[it % 2] = tr_done[it]
    fm_ready = p.val("dT")
    dump(p, "xt", XTa.rearrange("p a b -> p (a b)"), waits=[("sV", p.val("sV"))])
    dump(p, "lnb", LNB.rearrange("p a b -> p (a b)"), waits=[("sV", p.val("sV"))])
    dump(p, "stl", p.env["stats"], waits=[("sV", p.val("sV"))])
    dump(p, "fm", FM.rearrange("p a b -> p (a b)"), waits=[("dT", fm_ready)])

    bank_free = {"psB0": 0, "psB1": 0, "psC0": 0, "psC1": 0}
    banks = {"psB0": psB0, "psB1": psB1, "psC0": psC0, "psC1": psC1}

    def proj_fm(dst, dst_dc, wsb, w_cols, kcs, bias_col, scale, n_ic=4, src=None):
        src_fm = FM if src is None else src
        for ic in range(n_ic):
            bk = "psB0" if ic % 2 == 0 else "psB1"
            ps = banks[bk]
            p.wait("tensor", "sV", bank_free[bk])
            for j, kc in enumerate(kcs):
                p.op("tensor", lambda t, ps=ps, kc=kc, ic=ic, j=j: t.matmul(
                    ps, wsb[:, kc, w_cols], src_fm[:, kc, ic * 512:(ic + 1) * 512],
                    start=(j == 0), stop=(j == len(kcs) - 1)),
                    sem="sT" if j == len(kcs) - 1 else None)
            mv = p.val("sT")
            p.wait("vector", "sT", mv)
            if bias_col is not None:
                p.op("vector", lambda v, ps=ps, ic=ic: v.tensor_scalar(
                    dst[:, dst_dc, ic * 512:(ic + 1) * 512], ps, bias_col, scale,
                    ALU.add, ALU.mult), sem="sV")
            elif scale != 1.0:
                p.op("vector", lambda v, ps=ps, ic=ic: v.tensor_scalar_mul(
                    dst[:, dst_dc, ic * 512:(ic + 1) * 512], ps, scale), sem="sV")
            else:
                p.op("vector", lambda v, ps=ps, ic=ic: v.tensor_copy(
                    dst[:, dst_dc, ic * 512:(ic + 1) * 512], ps), sem="sV")
            bank_free[bk] = p.val("sV")

    p.wait("tensor", "dT", fm_ready)
    p.wait("tensor", "dW", wpre["Wo"])
    p.wait("tensor", "dX", wload)
    p.wait("scalar", "sV", bias_done)
    for dc2 in range(2):
        proj_fm(LAT, dc2, WL, slice(dc2 * 128, (dc2 + 1) * 128), range(8),
                BPP[:, 8 + dc2:9 + dc2], 1.0)

    astat = LNStat(env)
    last_proj_mm = [0]
    attn_barrier = [0, 0, 0]

    for g in range(4):
        p.wait("tensor", "sS", attn_barrier[0])
        p.wait("tensor", "sV", attn_barrier[1])
        for dc2 in range(2):
            proj_fm(QTG, dc2, WQ, slice(g * 256 + dc2 * 128, g * 256 + (dc2 + 1) * 128),
                    range(8), BPP[:, g * 2 + dc2:g * 2 + dc2 + 1], SCALE)
        for dc2 in range(2):
            proj_fm(KTG, dc2, WK, slice(g * 256 + dc2 * 128, g * 256 + (dc2 + 1) * 128),
                    range(2), BPP[:, 10 + g * 2 + dc2:11 + g * 2 + dc2], 1.0, src=LAT)
        astat.n = 0
        PF = carve(heap, 98304, BF16, [3 * T])
        BROW = carve(heap, 110592, BF16, [3 * NT * 128])[0:16, 0:T]
        BTv = carve(heap, 106496, BF16, [16, 16])
        NB = env["stats"][:, 128:192].rearrange("p (a b) -> p a b", a=16, b=4)
        prep_v, prep_s, prep_t = emit_bound_prep(
            p, env, QTG, KTG, env["sel64"], env["sel64"], PF, BROW, BTv, NB,
            astat, p.val("sV"), attn_barrier)
        for jt in range(NT):
            bk = "psC0" if jt % 2 == 0 else "psC1"
            ps = banks[bk]
            p.wait("tensor", "sV", bank_free[bk])
            for kc in range(2):
                p.op("tensor", lambda t, ps=ps, kc=kc, jt=jt: t.matmul(
                    ps[:, 0:256], LAT[:, kc, jt * 128:(jt + 1) * 128],
                    WV[:, kc, g * 256:(g + 1) * 256],
                    start=(kc == 0), stop=(kc == 1)), sem="sT" if kc == 1 else None)
            mv = p.val("sT")
            p.wait("vector", "sT", mv)
            p.op("vector", lambda v, ps=ps, jt=jt: v.tensor_tensor(
                VG[:, jt, :], ps[:, 0:256], BC1[:, g * 256:(g + 1) * 256], ALU.add), sem="sV")
            bank_free[bk] = p.val("sV")
        last_proj_mm[0] = p.val("sT")
        vg_ready = p.val("sV")
        if g == 0:
            dump(p, "lat", LAT.rearrange("p a b -> p (a b)"), waits=[("sV", p.val("sV"))])
            dump(p, "qtg", QTG.rearrange("p a b -> p (a b)"), waits=[("sV", p.val("sV"))])
            dump(p, "ktg", KTG.rearrange("p a b -> p (a b)"), waits=[("sV", p.val("sV"))])
            dump(p, "vg", VG.rearrange("p a b -> p (a b)"), waits=[("sV", p.val("sV"))])

        iters = [{"h": h, "qi": qi, "nw": T, "head": g * 4 + h,
                  "hc": h // 2, "hp": (h % 2) * 64, "diag_half": -1}
                 for h in range(4) for qi in range(NT)]

        def sc_fn(buf, it, lo, wdt):
            jcs = list(range(lo // 512, (lo + wdt) // 512))
            for jc in jcs:
                p.op("tensor", lambda t, jc=jc, it=it, buf=buf: t.matmul(
                    buf[:, jc * 512:(jc + 1) * 512],
                    QTG[it["hp"]:it["hp"] + 64, it["hc"], it["qi"] * 128:(it["qi"] + 1) * 128],
                    KTG[it["hp"]:it["hp"] + 64, it["hc"], jc * 512:(jc + 1) * 512],
                    start=True, stop=True), sem="sT" if jc == jcs[-1] else None)
            return p.val("sT")

        def av_fn(bufp, itp, ipp, tAv, tXv):
            p.wait("tensor", "dT", tAv)
            for kc in range(4, NT):
                p.op("tensor", lambda t, kc=kc, ipp=ipp, itp=itp, bufp=bufp: t.matmul(
                    bufp[:, 512:576], PT[:, ipp, kc, :], VG[:, kc, itp["h"] * 64:itp["h"] * 64 + 64],
                    start=(kc == 4), stop=False))
            p.wait("tensor", "dT", tXv)
            for kc in range(4):
                p.op("tensor", lambda t, kc=kc, ipp=ipp, itp=itp, bufp=bufp: t.matmul(
                    bufp[:, 512:576], PT[:, ipp, kc, :], VG[:, kc, itp["h"] * 64:itp["h"] * 64 + 64],
                    start=False, stop=(kc == 3)), sem="sT" if kc == 3 else None)
            return p.val("sT")

        def evac_fn(bufp, itp, rc):
            p.op("vector", lambda v, bufp=bufp, itp=itp, rc=rc: v.tensor_scalar_mul(
                O[:, itp["qi"], itp["head"] * 64:itp["head"] * 64 + 64],
                bufp[:, 512:576], rc), sem="sV")
            return p.val("sV")

        emit_attn_core(p, {"iters": iters, "np": 3, "P": P, "PT": PT, "astat": astat,
                           "scores": sc_fn, "av": av_fn, "evac": evac_fn,
                           "ready": prep_v, "ready_s": prep_s,
                           "bias": lambda it, NB=NB: NB[:, it["qi"], it["h"]:it["h"] + 1]})
        attn_barrier = [p.val("sS"), p.val("sV"), p.val("sT")]

    attn_all_done = p.val("sS")
    attn_all_v = p.val("sV")
    dump(p, "o", O.rearrange("p a b -> p (a b)"), waits=[("sV", attn_all_v)])

    p.wait("sync", "sS", attn_all_done)
    p.wait("sync", "sV", attn_all_v)
    p.wait("sync", "sT", last_proj_mm[0])
    ot_done = [0] * NT
    for it in range(NT):
        p.op("sync", lambda s, it=it: s.dma_start(
            out=FM[:, :, it * 128:(it + 1) * 128], in_=O[:, it, :],
            transpose=True), sem="dT", inc=16)
        ot_done[it] = p.val("dT")

    stats2 = LNStat(env)
    stats2.n = 0
    out_slot_free = [0, 0]
    store_done = [0, 0]
    xold_slot_free = [ln_done[14], ln_done[15]]
    for it in range(NT):
        s_out = XTa[:, 4 + it % 2, :]
        s_old = XTa[:, it % 2, :]
        scr = XTa[:, 2 + it % 2, :]
        p.wait("tensor", "dT", ot_done[it])
        for n in range(2):
            bk = "psB0" if n == 0 else "psB1"
            ps = banks[bk]
            p.wait("tensor", "sV", bank_free[bk])
            for kc in range(ND):
                p.op("tensor", lambda t, ps=ps, kc=kc, it=it, n=n: t.matmul(
                    ps, FM[:, kc, it * 128:(it + 1) * 128],
                    WO[:, kc, n * 512:(n + 1) * 512],
                    start=(kc == 0), stop=(kc == ND - 1)),
                    sem="sT" if kc == ND - 1 else None)
            mv = p.val("sT")
            p.wait("vector", "sT", mv)
            p.wait("vector", "sV", out_slot_free[it % 2])
            p.wait("vector", "dS", store_done[it % 2])
            p.op("vector", lambda v, ps=ps, n=n, s_out=s_out: v.tensor_tensor(
                s_out[:, n * 512:(n + 1) * 512], ps, BC2[:, n * 512:(n + 1) * 512],
                ALU.add), sem="sV")
            bank_free[bk] = p.val("sV")
        p.wait("scalar", "sV", xold_slot_free[it % 2])
        p.op("scalar", lambda g, it=it, s_old=s_old: g.dma_start(
            out=s_old, in_=xin[it * 128:(it + 1) * 128, :]), sem="dX", inc=16)
        xo = p.val("dX")
        u = scr
        if stats2.n > 240:
            stats2.n = 0
        p.wait("scalar", "sV", p.val("sV"))
        lnv = emit_ln_tile(p, s_out, u, scr, stats2, EPS_LN)
        p.wait("vector", "dX", xo)
        p.wait("vector", "sV", lnv)
        p.op("vector", lambda v, s_out=s_out, s_old=s_old, u=u: v.tensor_tensor(
            s_out, s_old, u, ALU.add), sem="sV")
        nv = p.val("sV")
        xold_slot_free[it % 2] = nv
        p.wait("scalar", "sV", nv)
        p.op("scalar", lambda g, it=it, s_out=s_out: g.dma_start(
            out=xout[it * 128:(it + 1) * 128, :], in_=s_out), sem="dS", inc=16)
        store_done[it % 2] = p.val("dS")
        out_slot_free[it % 2] = nv


BLOCK_WEIGHTS = {
    "l0_attn": ["L0_attn_Wq", "L0_attn_bq", "L0_attn_Wl", "L0_attn_bl",
                "L0_attn_Wk", "L0_attn_bk", "L0_attn_Wv", "L0_attn_bv",
                "L0_attn_Wo", "L0_attn_bo"],
    "l0_ffn": ["L0_ffn_g1", "L0_ffn_g2", "L0_ffn_Wq", "L0_ffn_Wkc", "L0_ffn_Wvc",
               "L0_ffn_Wo", "L0_ffn_Wr", "L0_ffn_expert_bias", "L0_ffn_sW1",
               "L0_ffn_sW2", "L0_ffn_sW3", "L0_ffn_eW1", "L0_ffn_eW2", "L0_ffn_eW3"],
    "l1_attn": ["L1_attn_Wq", "L1_attn_Wk", "L1_attn_Wv", "L1_attn_Wp", "L1_attn_bp"],
    "l1_ffn": ["L1_ffn_w1_w", "L1_ffn_w1_s", "L1_ffn_w2_w", "L1_ffn_w2_s",
               "L1_ffn_h1_w", "L1_ffn_h1_s", "L1_ffn_h2_w", "L1_ffn_h2_s",
               "L1_ffn_w3_w", "L1_ffn_w3_s"],
}

WEIGHT_SHAPES = {
    "L0_attn_Wq": [1024, 1024], "L0_attn_bq": [1024],
    "L0_attn_Wl": [1024, 256], "L0_attn_bl": [256],
    "L0_attn_Wk": [256, 1024], "L0_attn_bk": [1024],
    "L0_attn_Wv": [256, 1024], "L0_attn_bv": [1024],
    "L0_attn_Wo": [1024, 1024], "L0_attn_bo": [1024],
    "L0_ffn_g1": [1024], "L0_ffn_g2": [1024],
    "L0_ffn_Wq": [1024, 1024], "L0_ffn_Wkc": [256, 1024], "L0_ffn_Wvc": [256, 1024],
    "L0_ffn_Wo": [1024, 1024], "L0_ffn_Wr": [8, 1024], "L0_ffn_expert_bias": [8],
    "L0_ffn_sW1": [1024, 1024], "L0_ffn_sW2": [1024, 1024], "L0_ffn_sW3": [1024, 1024],
    "L0_ffn_eW1": [8, 1024, 1024], "L0_ffn_eW2": [8, 1024, 1024], "L0_ffn_eW3": [8, 1024, 1024],
    "L1_attn_Wq": [1024, 1024], "L1_attn_Wk": [1024, 1024], "L1_attn_Wv": [1024, 1024],
    "L1_attn_Wp": [1024, 1024], "L1_attn_bp": [1024],
    "L1_ffn_w1_w": [4096, 1024], "L1_ffn_w1_s": [4096],
    "L1_ffn_w2_w": [4096, 1024], "L1_ffn_w2_s": [4096],
    "L1_ffn_h1_w": [4096, 4096], "L1_ffn_h1_s": [4096],
    "L1_ffn_h2_w": [4096, 4096], "L1_ffn_h2_s": [4096],
    "L1_ffn_w3_w": [1024, 4096], "L1_ffn_w3_s": [1024],
}

BLOCK_SCRATCH = {
    "l0_attn": {},
    "l0_ffn": {
        "wq": ([1024, 1024], BF16), "wkc": ([256, 1024], BF16),
        "wvc": ([256, 1024], BF16), "wo": ([1024, 1024], BF16),
        "wr": ([16, 1024], BF16),
        "sw1": ([1024, 1024], BF16), "sw2": ([1024, 1024], BF16),
        "sw3": ([1024, 1024], BF16),
        "ew1": ([8, 1024, 1024], BF16), "ew2": ([8, 1024, 1024], BF16),
        "ew3": ([8, 1024, 1024], BF16),
        "Ad": ([T, D], F32), "Xid": ([T, D], F32),
    },
    "l1_attn": {
        "q": ([1024, 1024], BF16), "k": ([1024, 1024], BF16),
        "v": ([1024, 1024], BF16), "p": ([1024, 1024], BF16),
    },
    "l1_ffn": {
        "w1": ([4096, 1024], BF16), "w2": ([4096, 1024], BF16),
        "h1": ([4096, 4096], BF16), "h2": ([4096, 4096], BF16),
        "w3": ([1024, 4096], BF16),
    },
}


def _short(name):
    return name.split("_", 2)[2]


def emit_barrier(p):
    vals = {s: p.val(s) for s in p.env["sems"]}
    for e in ENGINES:
        for s, v in vals.items():
            p.wait(e, s, v)


def emit_prefetch(p, blocks, dram):
    wpre = {}
    for bi, b in enumerate(blocks):
        d = {}
        if b == "l0_attn":
            heap = p.env["heap"]
            WQ = carve(heap, 151552, BF16, [ND, D])
            WL = carve(heap, 167936, BF16, [ND, 256])
            WK = carve(heap, 172032, BF16, [2, D])
            WV = carve(heap, 176128, BF16, [2, D])
            WO = carve(heap, 180224, BF16, [ND, D])
            for dst, key in [(WQ, "Wq"), (WL, "Wl"), (WK, "Wk"), (WV, "Wv"), (WO, "Wo")]:
                d[key] = p.op("gpsimd", lambda g, dst=dst, key=key: g.dma_start(
                    out=dst, in_=dram[f"L0_attn_{key}"].rearrange("(c p) f -> p c f", p=128)),
                    sem="dW", inc=16)
        elif b == "l0_ffn":
            for nm_, key in [("wq", "Wq"), ("wkc", "Wkc"), ("wvc", "Wvc"), ("wo", "Wo")]:
                d[nm_] = p.op("gpsimd", lambda g, nm_=nm_, key=key, bi=bi: g.dma_start(
                    out=dram[f"s{bi}_{nm_}"], in_=dram[f"L0_ffn_{key}"]), sem="dW", inc=16)
            d["wr"] = p.op("gpsimd", lambda g, bi=bi: g.dma_start(
                out=dram[f"s{bi}_wr"][0:8, :], in_=dram["L0_ffn_Wr"]), sem="dW", inc=16)
            p.wait("gpsimd", "dT", 16 * 16)
            for nm_, key in [("sw1", "sW1"), ("sw2", "sW2"), ("sw3", "sW3"),
                             ("ew1", "eW1"), ("ew2", "eW2"), ("ew3", "eW3")]:
                d[nm_] = p.op("gpsimd", lambda g, nm_=nm_, key=key, bi=bi: g.dma_start(
                    out=dram[f"s{bi}_{nm_}"], in_=dram[f"L0_ffn_{key}"]), sem="dW", inc=16)
        elif b == "l1_attn":
            for nm_, key in [("q", "Wq"), ("k", "Wk"), ("v", "Wv"), ("p", "Wp")]:
                d[nm_] = p.op("gpsimd", lambda g, nm_=nm_, key=key, bi=bi: g.dma_start(
                    out=dram[f"s{bi}_{nm_}"], in_=dram[f"L1_attn_{key}"]), sem="dW", inc=16)
        elif b == "l1_ffn":
            for nm_, key in [("w1", "w1_w"), ("w2", "w2_w"), ("h1", "h1_w"),
                             ("h2", "h2_w"), ("w3", "w3_w")]:
                d[nm_] = p.op("gpsimd", lambda g, nm_=nm_, key=key, bi=bi: g.dma_start(
                    out=dram[f"s{bi}_{nm_}"], in_=dram[f"L1_ffn_{key}"]), sem="dW", inc=16)
        wpre[bi] = d
    return wpre


def emit_program(p, blocks, dram):
    env = p.env
    emit_consts(p)
    env["consts_done"] = p.val("sG")
    wpre = emit_prefetch(p, blocks, dram)
    n = len(blocks)
    for bi, b in enumerate(blocks):
        if bi > 0:
            emit_barrier(p)
        xin = dram["xin"] if bi == 0 else dram[f"xmid{bi - 1}"]
        xout = dram["xout"] if bi == n - 1 else dram[f"xmid{bi}"]
        w = {_short(k): dram[k] for k in BLOCK_WEIGHTS[b]}
        scr_d = {k: dram[f"s{bi}_{k}"] for k in BLOCK_SCRATCH[b]}
        try:
            if b == "l0_attn":
                emit_l0_attn(p, w, xin, xout, wpre[bi])
            elif b == "l0_ffn":
                emit_l0_ffn(p, w, scr_d, xin, xout, wpre[bi])
            elif b == "l1_attn":
                emit_l1_attn(p, w, scr_d, xin, xout, wpre[bi])
            elif b == "l1_ffn":
                emit_l1_ffn(p, w, scr_d, xin, xout, wpre[bi])
            else:
                raise ValueError(b)
        except StopEmission:
            break
    p.wait("gpsimd", "dS", p.val("dS"))


def build_nc(blocks, dbg=None, stop=None):
    nc = bass.Bass("TRN2", target_bir_lowering=False)
    dram = {}
    dram["xin"] = nc.dram_tensor("xin", [T, D], F32, kind="ExternalInput").ap()
    dram["xout"] = nc.dram_tensor("xout", [T, D], F32, kind="ExternalOutput").ap()
    for bi in range(len(blocks) - 1):
        dram[f"xmid{bi}"] = nc.dram_tensor(f"xmid{bi}", [T, D], F32, kind="Internal").ap()
    for bi, b in enumerate(blocks):
        for wname in BLOCK_WEIGHTS[b]:
            dram[wname] = nc.dram_tensor(
                wname, WEIGHT_SHAPES[wname], F32, kind="ExternalInput").ap()
        for sname, (shape, dt_) in BLOCK_SCRATCH[b].items():
            dram[f"s{bi}_{sname}"] = nc.dram_tensor(
                f"s{bi}_{sname}", shape, dt_, kind="Internal").ap()
    dbg_aps = {}
    for name, (shape, dt_) in (dbg or {}).items():
        dbg_aps[name] = nc.dram_tensor("dbg_" + name, shape, dt_, kind="ExternalOutput").ap()

    with contextlib.ExitStack() as stack:
        e = stack.enter_context
        heap_h = e(nc.sbuf_tensor("heap", [128, HEAP_BYTES // 4], F32))
        small_h = e(nc.sbuf_tensor("small", [128, SMALL_F32], F32))
        psA = e(nc.psum_tensor("psA", [128, 2048], F32))
        psA2 = e(nc.psum_tensor("psA2", [128, 2048], F32))
        sems = {}
        for s in ("dW", "dX", "dT", "dS", "sT", "sV", "sS", "sG"):
            sems[s] = e(nc.semaphore(s))
        blk = e(nc.Block())

        heap = heap_h[:, :]
        small = small_h[:, :]
        env = {
            "nc": nc,
            "sems": sems,
            "heap": heap,
            "small": small,
            "psA": psA[:, :],
            "psA2": psA2[:, :],
            "psB0": psA2[:, 0:512],
            "psB1": psA2[:, 512:1024],
            "psC0": psA2[:, 1024:1536],
            "psC1": psA2[:, 1536:2048],
            "stats": small[:, 2080:2336],
            "ident": small[:, 2336:2464],
            "mask": small[:, 2464:2592],
            "ones_row": small[0:1, 2592:2720],
            "ones_row_bf": small[0:1, 2720:2784].bitcast(BF16),
            "ones_col_bf": small[:, 2916:2917].bitcast(BF16)[:, 0:1],
            "sel64": [small[:, 2920 + 8 * c:2928 + 8 * c].bitcast(BF16) for c in range(2)],
            "self4": [small[:, 2940 + 8 * m:2948 + 8 * m].bitcast(BF16) for m in range(4)],
            "sall4": small[:, 2975:2983].bitcast(BF16),
            "consts_done": 0,
            "dbg": dbg_aps,
            "stop": stop,
        }

        def mk_body(ename):
            def body(eng):
                p = Prog(env, ename, eng)
                emit_program(p, blocks, dram)
            return body

        blk.tensor(mk_body("tensor"))
        blk.vector(mk_body("vector"))
        blk.scalar(mk_body("scalar"))
        blk.gpsimd(mk_body("gpsimd"))
        blk.sync(mk_body("sync"))
    return nc


_NC_CACHE = {}


def _get_nc(blocks):
    key = tuple(blocks)
    if key not in _NC_CACHE:
        _NC_CACHE[key] = build_nc(list(blocks))
    return _NC_CACHE[key]


ALL_BLOCKS = ["l0_attn", "l0_ffn", "l1_attn", "l1_ffn"]


FUSED = True


def kernel(**inputs):
    x = np.ascontiguousarray(np.asarray(inputs["L0_attn_x"], dtype=np.float32))
    cur = [x[b] for b in range(8)]
    groups = [ALL_BLOCKS] if FUSED else [[b] for b in ALL_BLOCKS]
    for grp in groups:
        nc = _get_nc(grp)
        wnames = [wn for b in grp for wn in BLOCK_WEIGHTS[b]]
        in_maps = []
        for c in range(8):
            m = {"xin": cur[c]}
            for wname in wnames:
                m[wname] = np.ascontiguousarray(np.asarray(inputs[wname], dtype=np.float32))
            in_maps.append(m)
        res = run_bass_kernel_spmd(nc, in_maps, core_ids=list(range(8)))
        cur = [np.asarray(r["xout"]) for r in res.results]
    return np.stack(cur).astype(np.float32)


def emit_l1_attn(p, w, scr_d, xin, xout, wpre):
    env = p.env
    heap = env["heap"]
    small = env["small"]
    FM = carve(heap, 0, BF16, [ND, T])
    QTG = carve(heap, 32768, BF16, [2, T])
    KTG = carve(heap, 40960, BF16, [2, T])
    VG = carve(heap, 49152, BF16, [NT, 256])
    WTQ = carve(heap, 57344, BF16, [ND, 256])
    WTK = carve(heap, 61440, BF16, [ND, 256])
    WTV = carve(heap, 65536, BF16, [ND, 256])
    O = carve(heap, 69632, BF16, [NT, D])
    P = carve(heap, 102400, BF16, [3, T])
    PT = carve(heap, 114688, BF16, [3, NT, 128])
    XTa = carve(heap, 126976, F32, [6, D])
    LNB = carve(heap, 151552, BF16, [2, D])
    WPT = carve(heap, 155648, BF16, [ND, D])
    BC2 = small[:, 1024:2048]
    mask = env["mask"]
    stats = LNStat(env)
    psA, psA2 = env["psA"], env["psA2"]
    banks = {"psB0": env["psB0"], "psB1": env["psB1"],
             "psC0": env["psC0"], "psC1": env["psC1"]}
    bank_free = {k: 0 for k in banks}
    SCALE = 0.125

    row_bp = XTa[0:1, 3, :]
    p.op("scalar", lambda g: g.dma_start(out=row_bp, in_=w["bp"].unsqueeze(0)), sem="dX", inc=16)
    wload = p.val("dX")
    emit_bcast(p, row_bp, D, BC2, wait=("dX", wload))
    bias_done = p.val("sV")

    ln_done = [0] * NT
    tr_done = [0] * NT
    slot_free = [0, 0]
    lnb_free = [0, 0]
    for it in range(NT):
        ds = XTa[:, it % 2, :]
        scr = XTa[:, 2 + it % 2, :]
        stats.n = (it % 2) * 112
        if it < 2:
            p.wait("scalar", "sV", bias_done)
        p.wait("scalar", "sV", slot_free[it % 2])
        p.op("scalar", lambda g, it=it, ds=ds: g.dma_start(
            out=ds, in_=xin[it * 128:(it + 1) * 128, :]), sem="dX", inc=16)
        xv = p.val("dX")
        p.wait("vector", "dX", xv)
        p.wait("scalar", "dX", xv)
        p.wait("vector", "dT", lnb_free[it % 2])
        ln_done[it] = emit_ln_tile(p, ds, LNB[:, it % 2, :], scr, stats, EPS_LN)
        slot_free[it % 2] = ln_done[it]
        p.wait("sync", "sV", ln_done[it])
        p.op("sync", lambda s, it=it: s.dma_start(
            out=FM[:, :, it * 128:(it + 1) * 128], in_=LNB[:, it % 2, :],
            transpose=True), sem="dT", inc=16)
        tr_done[it] = p.val("dT")
        lnb_free[it % 2] = tr_done[it]
    fm_ready = p.val("dT")
    p.wait("sync", "dW", wpre["p"])
    p.op("sync", lambda s: s.dma_start(out=WPT, in_=scr_d["p"], transpose=True), sem="dT", inc=16)
    wpt_done = p.val("dT")

    def proj_fm(dst, dst_dc, wsb, w_cols, kcs, scale, src=None):
        src_fm = FM if src is None else src
        for ic in range(4):
            bk = "psB0" if ic % 2 == 0 else "psB1"
            ps = banks[bk]
            p.wait("tensor", "sV", bank_free[bk])
            for j, kc in enumerate(kcs):
                p.op("tensor", lambda t, ps=ps, kc=kc, ic=ic, j=j: t.matmul(
                    ps, wsb[:, kc, w_cols], src_fm[:, kc, ic * 512:(ic + 1) * 512],
                    start=(j == 0), stop=(j == len(kcs) - 1)),
                    sem="sT" if j == len(kcs) - 1 else None)
            mv = p.val("sT")
            p.wait("vector", "sT", mv)
            if scale != 1.0:
                p.op("vector", lambda v, ps=ps, ic=ic: v.tensor_scalar_mul(
                    dst[:, dst_dc, ic * 512:(ic + 1) * 512], ps, scale), sem="sV")
            else:
                p.op("vector", lambda v, ps=ps, ic=ic: v.tensor_copy(
                    dst[:, dst_dc, ic * 512:(ic + 1) * 512], ps), sem="sV")
            bank_free[bk] = p.val("sV")

    astat = LNStat(env)
    attn_barrier = [0, 0, 0]
    last_gw_done = [0]
    last_proj_mm = [0]

    for g in range(4):
        p.wait("sync", "sT", last_proj_mm[0])
        p.wait("sync", "dW", wpre["v"])
        for wt, key in [(WTQ, "q"), (WTK, "k"), (WTV, "v")]:
            p.op("sync", lambda s, wt=wt, key=key: s.dma_start(
                out=wt, in_=scr_d[key][g * 256:(g + 1) * 256, :], transpose=True),
                sem="dT", inc=16)
        gw_done = p.val("dT")
        last_gw_done[0] = gw_done
        p.wait("tensor", "dT", fm_ready)
        p.wait("tensor", "dT", gw_done)
        p.wait("tensor", "sS", attn_barrier[0])
        p.wait("tensor", "sV", attn_barrier[1])
        for dc2 in range(2):
            proj_fm(QTG, dc2, WTQ, slice(dc2 * 128, (dc2 + 1) * 128), range(8), SCALE)
        for dc2 in range(2):
            proj_fm(KTG, dc2, WTK, slice(dc2 * 128, (dc2 + 1) * 128), range(8), 1.0)
        astat.n = 0
        PF = carve(heap, 102400, BF16, [3 * T])
        BROW = carve(heap, 114688, BF16, [3 * NT * 128])[0:16, 0:T]
        BTv = carve(heap, 110592, BF16, [16, 16])
        NB = env["stats"][:, 128:192].rearrange("p (a b) -> p a b", a=16, b=4)
        prep_v, prep_s, prep_t = emit_bound_prep(
            p, env, QTG, KTG, env["sel64"], env["sel64"], PF, BROW, BTv, NB,
            astat, p.val("sV"), attn_barrier)
        for jt in range(NT):
            bk = "psC0" if jt % 2 == 0 else "psC1"
            ps = banks[bk]
            p.wait("tensor", "sV", bank_free[bk])
            for kc in range(ND):
                p.op("tensor", lambda t, ps=ps, kc=kc, jt=jt: t.matmul(
                    ps[:, 0:256], FM[:, kc, jt * 128:(jt + 1) * 128],
                    WTV[:, kc, 0:256],
                    start=(kc == 0), stop=(kc == ND - 1)), sem="sT" if kc == ND - 1 else None)
            mv = p.val("sT")
            p.wait("vector", "sT", mv)
            p.op("vector", lambda v, ps=ps, jt=jt: v.tensor_copy(
                VG[:, jt, :], ps[:, 0:256]), sem="sV")
            bank_free[bk] = p.val("sV")
        last_proj_mm[0] = p.val("sT")
        vg_ready = p.val("sV")

        iters = []
        for h in range(4):
            for qi in range(NT):
                nw = (qi + 1) * 128
                iters.append({"h": h, "qi": qi, "nw": nw, "head": g * 4 + h,
                              "hc": h // 2, "hp": (h % 2) * 64,
                              "diag_part": 0 if qi < 4 else (1 if qi < 8 else 2)})

        def sc_fn(buf, it, lo, wdt):
            jcs = []
            o = lo
            while o < lo + wdt:
                w_ = min(512, lo + wdt - o)
                jcs.append((o, w_))
                o += w_
            for o, w_ in jcs:
                p.op("tensor", lambda t, o=o, w_=w_, it=it, buf=buf: t.matmul(
                    buf[:, o:o + w_],
                    QTG[it["hp"]:it["hp"] + 64, it["hc"], it["qi"] * 128:(it["qi"] + 1) * 128],
                    KTG[it["hp"]:it["hp"] + 64, it["hc"], o:o + w_],
                    start=True, stop=True), sem="sT" if (o, w_) == jcs[-1] else None)
            return p.val("sT")

        def mask_fn(buf, it):
            off = it["qi"] * 128
            mk = p.op("vector", lambda v, buf=buf, off=off: v.tensor_tensor(
                buf[:, off:off + 128], buf[:, off:off + 128], mask, ALU.add), sem="sV")
            return mk

        def av_fn(bufp, itp, ipp, tAv, tXv):
            nj = itp["qi"] + 1
            if nj > 4:
                p.wait("tensor", "dT", tAv)
                for kc in range(4, nj):
                    p.op("tensor", lambda t, kc=kc, ipp=ipp, itp=itp, bufp=bufp: t.matmul(
                        bufp[:, 512:576], PT[:, ipp, kc, :], VG[:, kc, itp["h"] * 64:itp["h"] * 64 + 64],
                        start=(kc == 4), stop=False))
            p.wait("tensor", "dT", tXv)
            for kc in range(min(nj, 4)):
                p.op("tensor", lambda t, kc=kc, ipp=ipp, itp=itp, bufp=bufp, nj=nj: t.matmul(
                    bufp[:, 512:576], PT[:, ipp, kc, :], VG[:, kc, itp["h"] * 64:itp["h"] * 64 + 64],
                    start=(nj <= 4 and kc == 0), stop=(kc == min(nj, 4) - 1)),
                    sem="sT" if kc == min(nj, 4) - 1 else None)
            return p.val("sT")

        def evac_fn(bufp, itp, rc):
            p.op("vector", lambda v, bufp=bufp, itp=itp, rc=rc: v.tensor_scalar_mul(
                O[:, itp["qi"], itp["head"] * 64:itp["head"] * 64 + 64],
                bufp[:, 512:576], rc), sem="sV")
            return p.val("sV")

        emit_attn_core(p, {"iters": iters, "np": 3, "P": P, "PT": PT, "astat": astat,
                           "scores": sc_fn, "av": av_fn, "evac": evac_fn,
                           "ready": prep_v, "ready_s": prep_s, "mask": mask_fn,
                           "bias": lambda it, NB=NB: NB[:, it["qi"], it["h"]:it["h"] + 1]})
        attn_barrier = [p.val("sS"), p.val("sV"), p.val("sT")]

    attn_all_done = p.val("sS")
    attn_all_v = p.val("sV")

    p.wait("sync", "sS", attn_all_done)
    p.wait("sync", "sV", attn_all_v)
    p.wait("sync", "sT", last_proj_mm[0])
    ot_done = [0] * NT
    for it in range(NT):
        p.op("sync", lambda s, it=it: s.dma_start(
            out=FM[:, :, it * 128:(it + 1) * 128], in_=O[:, it, :],
            transpose=True), sem="dT", inc=16)
        ot_done[it] = p.val("dT")

    stats2 = LNStat(env)
    out_slot_free = [0, 0]
    store_done = [0, 0]
    xold_slot_free = [ln_done[14], ln_done[15]]
    for it in range(NT):
        s_out = XTa[:, 4 + it % 2, :]
        s_old = XTa[:, it % 2, :]
        scr = XTa[:, 2 + it % 2, :]
        stats2.n = (it % 2) * 112
        p.wait("tensor", "dT", ot_done[it])
        p.wait("tensor", "dT", wpt_done)
        for n in range(2):
            bk = "psB0" if n == 0 else "psB1"
            ps = banks[bk]
            p.wait("tensor", "sV", bank_free[bk])
            for kc in range(ND):
                p.op("tensor", lambda t, ps=ps, kc=kc, it=it, n=n: t.matmul(
                    ps, FM[:, kc, it * 128:(it + 1) * 128],
                    WPT[:, kc, n * 512:(n + 1) * 512],
                    start=(kc == 0), stop=(kc == ND - 1)),
                    sem="sT" if kc == ND - 1 else None)
            mv = p.val("sT")
            p.wait("vector", "sT", mv)
            p.wait("vector", "sV", out_slot_free[it % 2])
            p.wait("vector", "dS", store_done[it % 2])
            p.op("vector", lambda v, ps=ps, n=n, s_out=s_out: v.tensor_tensor(
                s_out[:, n * 512:(n + 1) * 512], ps, BC2[:, n * 512:(n + 1) * 512],
                ALU.add), sem="sV")
            bank_free[bk] = p.val("sV")
        p.wait("scalar", "sV", xold_slot_free[it % 2])
        p.op("scalar", lambda g, it=it, s_old=s_old: g.dma_start(
            out=s_old, in_=xin[it * 128:(it + 1) * 128, :]), sem="dX", inc=16)
        xo = p.val("dX")
        u = scr
        p.wait("scalar", "sV", p.val("sV"))
        lnv = emit_ln_tile(p, s_out, u, scr, stats2, EPS_LN)
        p.wait("vector", "dX", xo)
        p.wait("vector", "sV", lnv)
        p.op("vector", lambda v, s_out=s_out, s_old=s_old, u=u: v.tensor_tensor(
            s_out, s_old, u, ALU.add), sem="sV")
        nv = p.val("sV")
        xold_slot_free[it % 2] = nv
        p.wait("scalar", "sV", nv)
        p.op("scalar", lambda g, it=it, s_out=s_out: g.dma_start(
            out=xout[it * 128:(it + 1) * 128, :], in_=s_out), sem="dS", inc=16)
        store_done[it % 2] = p.val("dS")
        out_slot_free[it % 2] = nv


def emit_l1_ffn(p, w, scr_d, xin, xout, wpre):
    env = p.env
    heap = env["heap"]
    small = env["small"]
    FM = carve(heap, 0, BF16, [ND, T])
    Y1 = carve(heap, 32768, BF16, [32, 1024])
    H1 = carve(heap, 98304, BF16, [32, 1024])
    XH = carve(heap, 98304, F32, [8, 1024])
    XTe = carve(heap, 131072, F32, [6, D])
    WC = [carve(heap, 163840, BF16, [32, 128]),
          carve(heap, 172032, BF16, [32, 128])]
    W3C = carve(heap, 163840, BF16, [32, 256])
    YC1 = [carve(heap, 180224, BF16, [ND, 128]), carve(heap, 182272, BF16, [ND, 128])]
    YC2 = [carve(heap, 184320, BF16, [ND, 128]), carve(heap, 186368, BF16, [ND, 128])]
    TMP = [carve(heap, 188416, F32, [512]), carve(heap, 190464, F32, [512])]
    LNB = carve(heap, 192512, BF16, [2, D])
    XTa = carve(heap, 32768, F32, [6, D])
    S3B = small[:, 0:1024]
    SPP = {"s1": small[:, 1024:1056], "s2": small[:, 1056:1088],
           "h1": small[:, 1088:1120], "h2": small[:, 1120:1152]}
    stats = LNStat(env)
    banks = {"psB0": env["psB0"], "psB1": env["psB1"],
             "psC0": env["psC0"], "psC1": env["psC1"]}
    bank_free = {k: ("sV", 0) for k in banks}

    rows = {}
    for i, key in enumerate(["w1_s", "w2_s", "h1_s", "h2_s"]):
        r = XTa[0:32, i, 0:128]
        rows[key] = r
        p.op("scalar", lambda g, r=r, key=key: g.dma_start(
            out=r, in_=w[key].rearrange("(c p) -> c p", p=128)), sem="dX", inc=16)
    row_s3 = XTa[0:1, 4, :]
    p.op("scalar", lambda g: g.dma_start(out=row_s3, in_=w["w3_s"].unsqueeze(0)), sem="dX", inc=16)
    wload = p.val("dX")
    emit_bias_pp(p, rows["w1_s"], 32, SPP["s1"], wait=("dX", wload), psum=env["psB0"])
    emit_bias_pp(p, rows["w2_s"], 32, SPP["s2"], psum=env["psB1"])
    emit_bias_pp(p, rows["h1_s"], 32, SPP["h1"], psum=env["psC0"])
    emit_bias_pp(p, rows["h2_s"], 32, SPP["h2"], psum=env["psC1"])
    emit_bcast(p, row_s3, D, S3B)
    pre_done = p.val("sV")

    ln_done = [0] * NT
    slot_free = [0, 0]
    lnb_free = [0, 0]
    for it in range(NT):
        ds = XTa[:, it % 2, :]
        scr = XTa[:, 2 + it % 2, :]
        stats.n = (it % 2) * 112
        if it < 2:
            p.wait("scalar", "sV", pre_done)
        p.wait("scalar", "sV", slot_free[it % 2])
        p.op("scalar", lambda g, it=it, ds=ds: g.dma_start(
            out=ds, in_=xin[it * 128:(it + 1) * 128, :]), sem="dX", inc=16)
        xv = p.val("dX")
        p.wait("vector", "dX", xv)
        p.wait("scalar", "dX", xv)
        p.wait("vector", "dT", lnb_free[it % 2])
        ln_done[it] = emit_ln_tile(p, ds, LNB[:, it % 2, :], scr, stats, EPS_LN)
        slot_free[it % 2] = ln_done[it]
        p.wait("sync", "sV", ln_done[it])
        p.op("sync", lambda s, it=it: s.dma_start(
            out=FM[:, :, it * 128:(it + 1) * 128], in_=LNB[:, it % 2, :],
            transpose=True), sem="dT", inc=16)
        lnb_free[it % 2] = p.val("dT")
    fm_ready = p.val("dT")

    stats2 = LNStat(env)
    store_done = [0, 0]
    xold_free = [0, 0]
    uslot_free = [0, 0]
    ep_count = [0]
    tmp_free = [0, 0]

    for H in range(2):
        tok0 = H * 1024
        g0 = p.val("sT")
        yc_free = [g0, g0]
        for fo in range(32):
            b = fo % 2
            if fo < 2:
                p.wait("sync", "dW", wpre["w2"])
            p.wait("sync", "sT", yc_free[b])
            p.op("sync", lambda s, b=b, fo=fo: s.dma_start(
                out=YC1[b], in_=scr_d["w1"][fo * 128:(fo + 1) * 128, :],
                transpose=True), sem="dT", inc=16)
            p.op("sync", lambda s, b=b, fo=fo: s.dma_start(
                out=YC2[b], in_=scr_d["w2"][fo * 128:(fo + 1) * 128, :],
                transpose=True), sem="dT", inc=16)
            ycv = p.val("dT")
            if fo < 2:
                p.wait("tensor", "dT", fm_ready)
            p.wait("tensor", "dT", ycv)
            for ic in range(2):
                bka = "psB0" if ic % 2 == 0 else "psB1"
                bkb = "psC0" if ic % 2 == 0 else "psC1"
                psa_, psb_ = banks[bka], banks[bkb]
                p.wait("tensor", *bank_free[bka])
                for kc in range(ND):
                    p.op("tensor", lambda t, psa_=psa_, kc=kc, ic=ic, b=b: t.matmul(
                        psa_, YC1[b][:, kc, :], FM[:, kc, tok0 + ic * 512:tok0 + (ic + 1) * 512],
                        start=(kc == 0), stop=(kc == ND - 1)),
                        sem="sT" if kc == ND - 1 else None)
                amm = p.val("sT")
                p.wait("tensor", *bank_free[bkb])
                for kc in range(ND):
                    p.op("tensor", lambda t, psb_=psb_, kc=kc, ic=ic, b=b: t.matmul(
                        psb_, YC2[b][:, kc, :], FM[:, kc, tok0 + ic * 512:tok0 + (ic + 1) * 512],
                        start=(kc == 0), stop=(kc == ND - 1)),
                        sem="sT" if kc == ND - 1 else None)
                bmm = p.val("sT")
                p.wait("scalar", "sT", amm)
                p.wait("scalar", "sV", tmp_free[ic])
                p.op("scalar", lambda s, psa_=psa_, ic=ic, fo=fo: s.activation(
                    TMP[ic], psa_, AF.Silu, bias=0.0, scale=SPP["s1"][:, fo:fo + 1]), sem="sS")
                sil = p.val("sS")
                bank_free[bka] = ("sS", sil)
                p.wait("vector", "sT", bmm)
                p.wait("vector", "sS", sil)
                p.op("vector", lambda v, psb_=psb_, ic=ic, fo=fo: v.scalar_tensor_tensor(
                    Y1[:, fo, ic * 512:(ic + 1) * 512], psb_, SPP["s2"][:, fo:fo + 1],
                    TMP[ic], ALU.mult, ALU.mult), sem="sV")
                bank_free[bkb] = ("sV", p.val("sV"))
                tmp_free[ic] = p.val("sV")
            yc_free[b] = p.val("sT")
        y_done_v = p.val("sV")

        def h_phase(src, dst, wkey, spp, prev_done):
            g0h = p.val("sT")
            wc_free = [g0h, g0h]
            first = [True]
            for fo in range(32):
                b = fo % 2
                if fo < 2:
                    p.wait("sync", "dW", wpre[wkey])
                p.wait("sync", "sT", wc_free[b])
                p.op("sync", lambda s, b=b, fo=fo: s.dma_start(
                    out=WC[b], in_=scr_d[wkey][fo * 128:(fo + 1) * 128, :],
                    transpose=True), sem="dT", inc=16)
                wcv = p.val("dT")
                p.wait("tensor", "dT", wcv)
                for ic in range(2):
                    bk = "psB0" if ic % 2 == 0 else "psB1"
                    ps = banks[bk]
                    p.wait("tensor", *bank_free[bk])
                    if first[0]:
                        if prev_done:
                            p.wait("tensor", "sV", prev_done)
                        first[0] = False
                    for kc in range(32):
                        p.op("tensor", lambda t, ps=ps, kc=kc, ic=ic, b=b: t.matmul(
                            ps, WC[b][:, kc, :], src[:, kc, ic * 512:(ic + 1) * 512],
                            start=(kc == 0), stop=(kc == 31)),
                            sem="sT" if kc == 31 else None)
                    mv = p.val("sT")
                    p.wait("scalar", "sT", mv)
                    p.op("scalar", lambda s, ps=ps, ic=ic, fo=fo: s.activation(
                        dst[:, fo, ic * 512:(ic + 1) * 512], ps, AF.Gelu,
                        bias=0.0, scale=spp[:, fo:fo + 1]), sem="sS")
                    bank_free[bk] = ("sS", p.val("sS"))
                wc_free[b] = p.val("sT")
            return p.val("sS")

        h1_done = h_phase(Y1, H1, "h1", SPP["h1"], y_done_v)
        p.wait("tensor", "sS", h1_done)
        h2_done = h_phase(H1, Y1, "h2", SPP["h2"], 0)

        p.wait("sync", "sS", h2_done)
        p.wait("sync", "dW", wpre["w3"])
        xh_ready = [0] * 8
        for wc in range(4):
            p.op("sync", lambda s, wc=wc: s.dma_start(
                out=W3C, in_=scr_d["w3"][wc * 256:(wc + 1) * 256, :],
                transpose=True), sem="dT", inc=16)
            w3v = p.val("dT")
            p.wait("tensor", "dT", w3v)
            last_mm_this_wc = 0
            for sub in range(8):
                bk = ("psB0", "psB1", "psC0", "psC1")[sub % 4]
                ps = banks[bk]
                p.wait("tensor", *bank_free[bk])
                if wc == 0 and sub < 2:
                    p.wait("tensor", "sS", h2_done)
                for kc in range(32):
                    p.op("tensor", lambda t, ps=ps, kc=kc, sub=sub: t.matmul(
                        ps[:, 0:256], Y1[:, kc, sub * 128:(sub + 1) * 128],
                        W3C[:, kc, 0:256],
                        start=(kc == 0), stop=(kc == 31)),
                        sem="sT" if kc == 31 else None)
                mv = p.val("sT")
                last_mm_this_wc = mv
                p.wait("vector", "sT", mv)
                if wc == 0:
                    p.wait("vector", "dS", store_done[0] if sub % 2 == 0 else store_done[1])
                p.op("vector", lambda v, ps=ps, sub=sub, wc=wc: v.tensor_tensor(
                    XH[:, sub, wc * 256:(wc + 1) * 256], ps[:, 0:256],
                    S3B[:, wc * 256:(wc + 1) * 256], ALU.mult), sem="sV")
                bank_free[bk] = ("sV", p.val("sV"))
                if wc == 3:
                    xh_ready[sub] = p.val("sV")
            p.wait("sync", "sT", last_mm_this_wc)

        for sub in range(8):
            it = H * 8 + sub
            ep = ep_count[0]
            ep_count[0] += 1
            s_out = XH[:, sub, :]
            s_old = XTe[:, ep % 2, :]
            scr = XTe[:, 2 + ep % 2, :]
            u = XTe[:, 4 + ep % 2, :]
            stats2.n = (ep % 2) * 112
            if sub < 2:
                p.wait("scalar", "sV", xh_ready[sub])
            p.wait("scalar", "sV", xold_free[ep % 2])
            p.op("scalar", lambda g, it=it, s_old=s_old: g.dma_start(
                out=s_old, in_=xin[it * 128:(it + 1) * 128, :]), sem="dX", inc=16)
            xo = p.val("dX")
            p.wait("scalar", "sV", xh_ready[sub])
            p.wait("vector", "sV", xh_ready[sub])
            p.wait("scalar", "sV", uslot_free[ep % 2])
            lnv = emit_ln_tile(p, s_out, u, scr, stats2, EPS_LN)
            p.wait("vector", "dX", xo)
            p.wait("vector", "sV", lnv)
            p.wait("vector", "dS", store_done[ep % 2])
            p.op("vector", lambda v, s_out=s_out, s_old=s_old, u=u: v.tensor_tensor(
                s_out, s_old, u, ALU.add), sem="sV")
            nv = p.val("sV")
            xold_free[ep % 2] = nv
            uslot_free[ep % 2] = nv
            p.wait("scalar", "sV", nv)
            p.op("scalar", lambda g, it=it, s_out=s_out: g.dma_start(
                out=xout[it * 128:(it + 1) * 128, :], in_=s_out), sem="dS", inc=16)
            store_done[ep % 2] = p.val("dS")


def emit_l0_ffn(p, w, scr_d, xin, xout, wpre):
    env = p.env
    heap = env["heap"]
    small = env["small"]
    FM = carve(heap, 0, BF16, [ND, T])
    ACC_A = carve(heap, 0, F32, [ND, D])
    QT = carve(heap, 32768, BF16, [ND, T])
    NX2T = QT
    KCT = carve(heap, 65536, BF16, [2, T])
    VC = carve(heap, 73728, BF16, [NT, 256])
    EW1 = [carve(heap, 65536, BF16, [ND, D]),
           carve(heap, 114688, BF16, [ND, D])]
    O = carve(heap, 81920, BF16, [NT, D])
    ACC_B = carve(heap, 81920, F32, [ND, D])
    P = carve(heap, 114688, BF16, [2, T])
    PT = carve(heap, 122880, BF16, [2, NT, 128])
    XTa = carve(heap, 131072, F32, [6, D])
    EW2 = carve(heap, 131072, BF16, [ND, D])
    TMP2 = [carve(heap, 147456, F32, [512]), carve(heap, 149504, F32, [512])]
    LNB = carve(heap, 155648, BF16, [2, D])
    WQT = carve(heap, 159744, BF16, [ND, D])
    WOT = WQT
    EW3 = carve(heap, 159744, BF16, [ND, D])
    WKCT = carve(heap, 176128, BF16, [ND, 256])
    WVCT = carve(heap, 180224, BF16, [ND, 256])
    WRT = carve(heap, 184320, BF16, [ND, 16])
    HET = carve(heap, 184576, BF16, [ND, 512])
    RT = carve(heap, 192768, F32, [2, 5, 8])

    BC1 = small[:, 0:1024]
    BC2 = small[:, 1024:2048]
    EBB = small[:, 2048:2056]
    C = small[:, 2784:2912].rearrange("p (a b) -> p a b", a=16, b=8)
    stats = LNStat(env)
    psA, psA2 = env["psA"], env["psA2"]
    banks = {"psB0": env["psB0"], "psB1": env["psB1"],
             "psC0": env["psC0"], "psC1": env["psC1"]}
    bank_free = {k: ("sV", 0) for k in banks}

    def ACC(it):
        return (ACC_A if it < 8 else ACC_B)[:, it % 8, :]

    row_g1 = XTa[0:1, 3, :]
    row_g2 = XTa[0:1, 4, :]
    row_eb = XTa[0:1, 5, 0:8]
    p.op("scalar", lambda g: g.dma_start(out=row_g1, in_=w["g1"].unsqueeze(0)), sem="dX", inc=16)
    p.op("scalar", lambda g: g.dma_start(out=row_g2, in_=w["g2"].unsqueeze(0)), sem="dX", inc=16)
    p.op("scalar", lambda g: g.dma_start(out=row_eb, in_=w["expert_bias"].unsqueeze(0)), sem="dX", inc=16)
    wload = p.val("dX")
    emit_bcast(p, row_g1, D, BC1, wait=("dX", wload))
    emit_bcast(p, row_g2, D, BC2)
    emit_bcast(p, row_eb, 8, EBB)
    pre_done = p.val("sV")

    ln_done = [0] * NT
    slot_free = [0, 0]
    sa_free = [0, 0]
    lnb_free = [0, 0]
    ad_store = [0] * NT
    for it in range(NT):
        ds = XTa[:, it % 2, :]
        scr = XTa[:, 2 + it % 2, :]
        sA = XTa[:, 4 + it % 2, :]
        stats.n = (it % 2) * 112
        if it < 2:
            p.wait("scalar", "sV", pre_done)
        p.wait("scalar", "sV", slot_free[it % 2])
        p.op("scalar", lambda g, it=it, ds=ds: g.dma_start(
            out=ds, in_=xin[it * 128:(it + 1) * 128, :]), sem="dX", inc=16)
        xv = p.val("dX")
        p.wait("vector", "dX", xv)
        p.wait("scalar", "dX", xv)
        p.wait("vector", "dS", sa_free[it % 2])
        lnv = emit_ln_tile(p, ds, sA, scr, stats, EPS_LN)
        slot_free[it % 2] = lnv
        p.wait("scalar", "sV", lnv)
        p.op("scalar", lambda g, it=it, sA=sA: g.dma_start(
            out=scr_d["Ad"][it * 128:(it + 1) * 128, :], in_=sA), sem="dS", inc=16)
        ad_store[it] = p.val("dS")
        p.wait("scalar", "sV", lnv)
        p.wait("vector", "dT", lnb_free[it % 2])
        rmv = emit_ln_tile(p, sA, LNB[:, it % 2, :], scr, stats, EPS_RMS, gbc=BC1)
        sa_free[it % 2] = ad_store[it]
        ln_done[it] = rmv
        p.wait("sync", "sV", rmv)
        p.op("sync", lambda s, it=it: s.dma_start(
            out=FM[:, :, it * 128:(it + 1) * 128], in_=LNB[:, it % 2, :],
            transpose=True), sem="dT", inc=16)
        lnb_free[it % 2] = p.val("dT")
    fm_ready = p.val("dT")

    p.wait("sync", "dW", wpre["wvc"])
    p.op("sync", lambda s: s.dma_start(out=WQT, in_=scr_d["wq"], transpose=True), sem="dT", inc=16)
    p.op("sync", lambda s: s.dma_start(out=WKCT, in_=scr_d["wkc"], transpose=True), sem="dT", inc=16)
    p.op("sync", lambda s: s.dma_start(out=WVCT, in_=scr_d["wvc"], transpose=True), sem="dT", inc=16)
    wt_done = p.val("dT")

    def proj_fm(dst, dst_dc, wsb, w_cols, kcs, scale, src=None):
        src_fm = FM if src is None else src
        for ic in range(4):
            bk = "psB0" if ic % 2 == 0 else "psB1"
            ps = banks[bk]
            p.wait("tensor", *bank_free[bk])
            for j, kc in enumerate(kcs):
                p.op("tensor", lambda t, ps=ps, kc=kc, ic=ic, j=j: t.matmul(
                    ps, wsb[:, kc, w_cols], src_fm[:, kc, ic * 512:(ic + 1) * 512],
                    start=(j == 0), stop=(j == len(kcs) - 1)),
                    sem="sT" if j == len(kcs) - 1 else None)
            mv = p.val("sT")
            p.wait("vector", "sT", mv)
            if scale != 1.0:
                p.op("vector", lambda v, ps=ps, ic=ic: v.tensor_scalar_mul(
                    dst[:, dst_dc, ic * 512:(ic + 1) * 512], ps, scale), sem="sV")
            else:
                p.op("vector", lambda v, ps=ps, ic=ic: v.tensor_copy(
                    dst[:, dst_dc, ic * 512:(ic + 1) * 512], ps), sem="sV")
            bank_free[bk] = ("sV", p.val("sV"))

    p.wait("tensor", "dT", fm_ready)
    p.wait("tensor", "dT", wt_done)
    for dc in range(ND):
        proj_fm(QT, dc, WQT, slice(dc * 128, (dc + 1) * 128), range(ND), 1.0 / 16.0)
    qt_proj_last = p.val("sT")
    for dc2 in range(2):
        proj_fm(KCT, dc2, WKCT, slice(dc2 * 128, (dc2 + 1) * 128), range(ND), 1.0)
    astat = LNStat(env)
    PF = carve(heap, 114688, BF16, [2 * T])
    BROW = carve(heap, 122880, BF16, [2 * NT * 128])[0:16, 0:T]
    BTv = carve(heap, 118784, BF16, [16, 16])
    NB = env["stats"][:, 128:192].rearrange("p (a b) -> p a b", a=16, b=4)
    sels_q = [env["self4"][c // 2] for c in range(8)]
    sels_k = [env["sall4"], env["sall4"]]
    prep_v, prep_s, prep_t = emit_bound_prep(
        p, env, QT, KCT, sels_q, sels_k, PF, BROW, BTv, NB,
        astat, p.val("sV"), (p.val("sS"), p.val("sV"), p.val("sT")))
    for jt in range(NT):
        bk = "psC0" if jt % 2 == 0 else "psC1"
        ps = banks[bk]
        p.wait("tensor", *bank_free[bk])
        for kc in range(ND):
            p.op("tensor", lambda t, ps=ps, kc=kc, jt=jt: t.matmul(
                ps[:, 0:256], FM[:, kc, jt * 128:(jt + 1) * 128],
                WVCT[:, kc, 0:256],
                start=(kc == 0), stop=(kc == ND - 1)), sem="sT" if kc == ND - 1 else None)
        mv = p.val("sT")
        p.wait("vector", "sT", mv)
        p.op("vector", lambda v, ps=ps, jt=jt: v.tensor_copy(VC[:, jt, :], ps[:, 0:256]), sem="sV")
        bank_free[bk] = ("sV", p.val("sV"))
    proj_last = p.val("sT")
    vc_ready = p.val("sV")

    iters = [{"h": h, "qi": qi, "nw": T, "diag_half": -1}
             for h in range(4) for qi in range(NT)]

    def sc_fn(buf, it, lo, wdt):
        jcs = list(range(lo // 512, (lo + wdt) // 512))
        for jc in jcs:
            for kc2 in range(2):
                p.op("tensor", lambda t, jc=jc, kc2=kc2, it=it, buf=buf: t.matmul(
                    buf[:, jc * 512:(jc + 1) * 512],
                    QT[:, 2 * it["h"] + kc2, it["qi"] * 128:(it["qi"] + 1) * 128],
                    KCT[:, kc2, jc * 512:(jc + 1) * 512],
                    start=(kc2 == 0), stop=(kc2 == 1)),
                    sem="sT" if jc == jcs[-1] and kc2 == 1 else None)
        return p.val("sT")

    def av_fn(bufp, itp, ipp, tAv, tXv):
        p.wait("tensor", "dT", tAv)
        for kc in range(4, NT):
            p.op("tensor", lambda t, kc=kc, ipp=ipp, bufp=bufp: t.matmul(
                bufp[:, 512:768], PT[:, ipp, kc, :], VC[:, kc, :],
                start=(kc == 4), stop=False))
        p.wait("tensor", "dT", tXv)
        for kc in range(4):
            p.op("tensor", lambda t, kc=kc, ipp=ipp, bufp=bufp: t.matmul(
                bufp[:, 512:768], PT[:, ipp, kc, :], VC[:, kc, :],
                start=False, stop=(kc == 3)), sem="sT" if kc == 3 else None)
        return p.val("sT")

    def evac_fn(bufp, itp, rc):
        p.op("vector", lambda v, bufp=bufp, itp=itp, rc=rc: v.tensor_scalar_mul(
            O[:, itp["qi"], itp["h"] * 256:(itp["h"] + 1) * 256],
            bufp[:, 512:768], rc), sem="sV")
        return p.val("sV")

    emit_attn_core(p, {"iters": iters, "np": 2, "P": P, "PT": PT, "astat": astat,
                       "scores": sc_fn, "av": av_fn, "evac": evac_fn,
                       "ready": prep_v, "ready_s": prep_s,
                       "bias": lambda it: NB[:, it["qi"], it["h"]:it["h"] + 1]})
    attn_done_s = p.val("sS")
    attn_done_t = p.val("sT")
    attn_done_v = p.val("sV")

    p.wait("sync", "sS", attn_done_s)
    p.wait("sync", "sV", attn_done_v)
    p.wait("sync", "sT", proj_last)
    ot_done = [0] * NT
    for it in range(NT):
        p.op("sync", lambda s, it=it: s.dma_start(
            out=FM[:, :, it * 128:(it + 1) * 128], in_=O[:, it, :],
            transpose=True), sem="dT", inc=16)
        ot_done[it] = p.val("dT")
    p.wait("sync", "sT", qt_proj_last)
    p.wait("sync", "dW", wpre["wr"])
    p.op("sync", lambda s: s.dma_start(out=WOT, in_=scr_d["wo"], transpose=True), sem="dT", inc=16)
    p.op("sync", lambda s: s.dma_start(out=WRT, in_=scr_d["wr"], transpose=True), sem="dT", inc=16)
    wot_done = p.val("dT")

    xi_store = [0] * NT
    nx2_tr = [0] * NT
    aslot_free = [0, 0]
    xislot_free = [0, 0]
    for it in range(NT):
        s_a = XTa[:, it % 2, :]
        scr = XTa[:, 2 + it % 2, :]
        s_xi = XTa[:, 4 + it % 2, :]
        stats.n = (it % 2) * 112
        p.wait("tensor", "dT", ot_done[it])
        p.wait("tensor", "dT", wot_done)
        p.wait("scalar", "dS", ad_store[it])
        p.wait("scalar", "sV", aslot_free[it % 2])
        p.op("scalar", lambda g, it=it, s_a=s_a: g.dma_start(
            out=s_a, in_=scr_d["Ad"][it * 128:(it + 1) * 128, :]), sem="dX", inc=16)
        av_ = p.val("dX")
        for n in range(2):
            bk = "psB0" if n == 0 else "psB1"
            ps = banks[bk]
            p.wait("tensor", *bank_free[bk])
            for kc in range(ND):
                p.op("tensor", lambda t, ps=ps, kc=kc, it=it, n=n: t.matmul(
                    ps, FM[:, kc, it * 128:(it + 1) * 128],
                    WOT[:, kc, n * 512:(n + 1) * 512],
                    start=(kc == 0), stop=(kc == ND - 1)),
                    sem="sT" if kc == ND - 1 else None)
            mv = p.val("sT")
            p.wait("vector", "sT", mv)
            p.wait("vector", "dX", av_)
            p.wait("vector", "dS", xislot_free[it % 2])
            p.op("vector", lambda v, ps=ps, n=n, s_xi=s_xi, s_a=s_a: v.tensor_tensor(
                s_xi[:, n * 512:(n + 1) * 512], ps, s_a[:, n * 512:(n + 1) * 512],
                ALU.add), sem="sV")
            bank_free[bk] = ("sV", p.val("sV"))
        xiv = p.val("sV")
        aslot_free[it % 2] = xiv
        p.wait("scalar", "sV", xiv)
        p.op("scalar", lambda g, it=it, s_xi=s_xi: g.dma_start(
            out=scr_d["Xid"][it * 128:(it + 1) * 128, :], in_=s_xi), sem="dS", inc=16)
        xi_store[it] = p.val("dS")
        xislot_free[it % 2] = xi_store[it]
        p.wait("scalar", "sV", xiv)
        p.wait("vector", "dT", lnb_free[it % 2])
        rmv = emit_ln_tile(p, s_xi, LNB[:, it % 2, :], scr, stats, EPS_RMS, gbc=BC2)
        p.wait("sync", "sV", rmv)
        if it < 2:
            p.wait("sync", "sT", attn_done_t)
        p.op("sync", lambda s, it=it: s.dma_start(
            out=NX2T[:, :, it * 128:(it + 1) * 128], in_=LNB[:, it % 2, :],
            transpose=True), sem="dT", inc=16)
        nx2_tr[it] = p.val("dT")
        lnb_free[it % 2] = nx2_tr[it]
    nx2_ready = p.val("dT")

    m1c = [stats.misc(12), stats.misc(14)]
    m2c = [stats.misc(16), stats.misc(18)]
    e1c = [stats.misc(20), stats.misc(22)]
    e2c = [stats.misc(24), stats.misc(26)]
    d12 = [stats.misc(28), stats.misc(30)]
    for it in range(NT):
        par = it % 2
        lg = RT[:, par, 0, :]
        rw = RT[:, par, 1, :]
        rw2 = RT[:, par, 2, :]
        eq1 = RT[:, par, 3, :]
        eq2 = RT[:, par, 4, :]
        bk = "psC0" if par == 0 else "psC1"
        ps = banks[bk]
        p.wait("tensor", "dT", nx2_tr[it])
        p.wait("tensor", "dT", wot_done)
        p.wait("tensor", *bank_free[bk])
        for kc in range(ND):
            p.op("tensor", lambda t, ps=ps, kc=kc, it=it: t.matmul(
                ps[:, 0:8], NX2T[:, kc, it * 128:(it + 1) * 128], WRT[:, kc, 0:8],
                start=(kc == 0), stop=(kc == ND - 1)), sem="sT" if kc == ND - 1 else None)
        mv = p.val("sT")
        p.wait("vector", "sT", mv)
        vv = p.op("vector", lambda v, ps=ps, lg=lg: v.tensor_tensor(lg, ps[:, 0:8], EBB, ALU.add), sem="sV")
        bank_free[bk] = ("sV", vv)
        p.wait("vector", "sV", vv)
        vv = p.op("vector", lambda v, lg=lg, par=par: v.reduce_max(
            m1c[par], lg, axis=AX.X, negate=True), sem="sV")
        p.wait("scalar", "sV", vv)
        sv_ = p.op("scalar", lambda s, lg=lg, rw=rw, par=par: s.activation(
            rw, lg, AF.Exp, bias=m1c[par], scale=1.0, accum_out=d12[par]), sem="sS")
        p.wait("vector", "sS", sv_)
        vv = p.op("vector", lambda v, par=par: v.reciprocal(m2c[par], d12[par]), sem="sV")
        p.wait("vector", "sV", vv)
        vv = p.op("vector", lambda v, rw=rw, par=par: v.tensor_scalar_mul(
            rw, rw, m2c[par]), sem="sV")
        p.wait("vector", "sV", vv)
        vv = p.op("vector", lambda v, rw=rw, par=par: v.reduce_max(
            m1c[par], rw, axis=AX.X), sem="sV")
        p.wait("vector", "sV", vv)
        vv = p.op("vector", lambda v, rw=rw, eq1=eq1, par=par: v.tensor_scalar(
            eq1, rw, m1c[par], None, ALU.is_ge), sem="sV")
        p.wait("vector", "sV", vv)
        vv = p.op("vector", lambda v, rw=rw, rw2=rw2, eq1=eq1: v.scalar_tensor_tensor(
            rw2, eq1, NEG_BIG, rw, ALU.mult, ALU.add), sem="sV")
        p.wait("vector", "sV", vv)
        vv = p.op("vector", lambda v, rw2=rw2, par=par: v.reduce_max(
            m2c[par], rw2, axis=AX.X), sem="sV")
        p.wait("vector", "sV", vv)
        vv = p.op("vector", lambda v, rw2=rw2, eq2=eq2, par=par: v.tensor_scalar(
            eq2, rw2, m2c[par], None, ALU.is_ge), sem="sV")
        p.wait("scalar", "sV", vv)
        sv_ = p.op("scalar", lambda s, par=par: s.activation(
            e1c[par], m1c[par], AF.Exp, bias=0.0, scale=1.0), sem="sS")
        p.wait("scalar", "sS", sv_)
        sv_ = p.op("scalar", lambda s, par=par: s.activation(
            e2c[par], m2c[par], AF.Exp, bias=0.0, scale=1.0), sem="sS")
        p.wait("vector", "sS", sv_)
        vv = p.op("vector", lambda v, par=par: v.tensor_tensor(
            d12[par], e1c[par], e2c[par], ALU.add), sem="sV")
        p.wait("vector", "sV", vv)
        vv = p.op("vector", lambda v, par=par: v.reciprocal(m1c[par], d12[par]), sem="sV")
        p.wait("vector", "sV", vv)
        vv = p.op("vector", lambda v, par=par: v.tensor_tensor(
            e1c[par], e1c[par], m1c[par], ALU.mult), sem="sV")
        p.wait("vector", "sV", vv)
        vv = p.op("vector", lambda v, par=par: v.tensor_tensor(
            e2c[par], e2c[par], m1c[par], ALU.mult), sem="sV")
        p.wait("vector", "sV", vv)
        vv = p.op("vector", lambda v, eq1=eq1, par=par: v.tensor_scalar(
            eq1, eq1, e1c[par], None, ALU.mult), sem="sV")
        p.wait("vector", "sV", vv)
        vv = p.op("vector", lambda v, eq1=eq1, eq2=eq2, it=it, par=par: v.scalar_tensor_tensor(
            C[:, it, :], eq2, e2c[par], eq1, ALU.mult, ALU.add), sem="sV")
    router_done = p.val("sV")
    dump(p, "c", small[:, 2784:2912], waits=[("sV", router_done)])

    acc_init = [0] * NT
    for it in range(NT):
        p.wait("scalar", "dS", xi_store[it])
        if it == 0:
            p.wait("scalar", "sT", p.val("sT"))
            p.wait("scalar", "dT", ot_done[NT - 1])
        p.op("scalar", lambda g, it=it: g.dma_start(
            out=ACC(it), in_=scr_d["Xid"][it * 128:(it + 1) * 128, :]), sem="dX", inc=16)
        acc_init[it] = p.val("dX")
    acc_ready = p.val("dX")

    h_mm_last = [0, 0]
    e3_mm_last = [0]
    e2_mm_last = [0]
    het_evac = [0]
    out_evac_last = [0]
    tmp_free = [0, 0]
    for e in range(9):
        b = e % 2
        if e == 0:
            w1src, w2src, w3src = scr_d["sw1"], scr_d["sw2"], scr_d["sw3"]
        else:
            w1src = scr_d["ew1"][e - 1]
            w2src = scr_d["ew2"][e - 1]
            w3src = scr_d["ew3"][e - 1]
        if e == 0:
            p.wait("sync", "sS", attn_done_s)
            p.wait("sync", "sT", p.val("sT"))
            p.wait("sync", "dX", acc_ready)
            p.wait("sync", "sV", router_done)
            p.wait("sync", "dW", wpre["sw3"])
        if e == 1:
            p.wait("sync", "dW", wpre["ew3"])
        p.wait("sync", "sT", h_mm_last[b])
        p.op("sync", lambda s, b=b, w1src=w1src: s.dma_start(
            out=EW1[b], in_=w1src, transpose=True), sem="dT", inc=16)
        p.wait("sync", "sT", e3_mm_last[0])
        p.op("sync", lambda s, w3src=w3src: s.dma_start(
            out=EW3, in_=w3src, transpose=True), sem="dT", inc=16)
        w13_done = p.val("dT")
        p.wait("sync", "sT", e2_mm_last[0])
        p.op("sync", lambda s, w2src=w2src: s.dma_start(
            out=EW2, in_=w2src, transpose=True), sem="dT", inc=16)
        w2_done = p.val("dT")
        for it4 in range(4):
            p.wait("tensor", "dT", w13_done)
            if e == 0 and it4 == 0:
                p.wait("tensor", "dT", nx2_ready)
            for fc in range(ND):
                bka = "psB0" if fc % 2 == 0 else "psB1"
                bkb = "psC0" if fc % 2 == 0 else "psC1"
                psa_, psb_ = banks[bka], banks[bkb]
                p.wait("tensor", *bank_free[bka])
                for kc in range(ND):
                    p.op("tensor", lambda t, psa_=psa_, kc=kc, fc=fc, b=b, it4=it4: t.matmul(
                        psa_, EW1[b][:, kc, fc * 128:(fc + 1) * 128],
                        NX2T[:, kc, it4 * 512:(it4 + 1) * 512],
                        start=(kc == 0), stop=(kc == ND - 1)),
                        sem="sT" if kc == ND - 1 else None)
                amm = p.val("sT")
                p.wait("tensor", *bank_free[bkb])
                for kc in range(ND):
                    p.op("tensor", lambda t, psb_=psb_, kc=kc, fc=fc, it4=it4: t.matmul(
                        psb_, EW3[:, kc, fc * 128:(fc + 1) * 128],
                        NX2T[:, kc, it4 * 512:(it4 + 1) * 512],
                        start=(kc == 0), stop=(kc == ND - 1)),
                        sem="sT" if kc == ND - 1 else None)
                bmm = p.val("sT")
                h_mm_last[b] = bmm
                e3_mm_last[0] = bmm
                p.wait("scalar", "sT", amm)
                p.wait("scalar", "sV", tmp_free[fc % 2])
                p.op("scalar", lambda s, psa_=psa_, fc=fc: s.activation(
                    TMP2[fc % 2], psa_, AF.Silu, bias=0.0, scale=1.0), sem="sS")
                sil = p.val("sS")
                bank_free[bka] = ("sS", sil)
                p.wait("vector", "sT", bmm)
                p.wait("vector", "sS", sil)
                p.op("vector", lambda v, psb_=psb_, fc=fc: v.scalar_tensor_tensor(
                    HET[:, fc, :], psb_, 1.0, TMP2[fc % 2], ALU.mult, ALU.mult), sem="sV")
                hv = p.val("sV")
                bank_free[bkb] = ("sV", hv)
                tmp_free[fc % 2] = hv
            h_done = p.val("sV")
            p.wait("tensor", "dT", w2_done)
            for dn in range(2):
                for sub in range(4):
                    tg = it4 * 4 + sub
                    bk = ("psB0", "psB1", "psC0", "psC1")[(dn * 4 + sub) % 4]
                    ps = banks[bk]
                    p.wait("tensor", *bank_free[bk])
                    if dn == 0 and sub == 0:
                        p.wait("tensor", "sV", h_done)
                    for fc in range(ND):
                        p.op("tensor", lambda t, ps=ps, fc=fc, sub=sub, dn=dn: t.matmul(
                            ps, HET[:, fc, sub * 128:(sub + 1) * 128],
                            EW2[:, fc, dn * 512:(dn + 1) * 512],
                            start=(fc == 0), stop=(fc == ND - 1)),
                            sem="sT" if fc == ND - 1 else None)
                    mv = p.val("sT")
                    e2_mm_last[0] = mv
                    p.wait("vector", "sT", mv)
                    if e == 0 and it4 == 0 and dn == 0 and sub < 2:
                        p.wait("vector", "dX", acc_ready)
                        p.wait("vector", "sV", router_done)
                    if e == 0:
                        p.op("vector", lambda v, ps=ps, tg=tg, dn=dn: v.tensor_tensor(
                            ACC(tg)[:, dn * 512:(dn + 1) * 512],
                            ACC(tg)[:, dn * 512:(dn + 1) * 512], ps, ALU.add), sem="sV")
                    else:
                        tv_ = p.op("vector", lambda v, ps=ps, tg=tg, e=e: v.tensor_scalar_mul(
                            TMP2[tg % 2], ps, C[:, tg, e - 1:e]), sem="sV")
                        p.wait("vector", "sV", tv_)
                        p.op("vector", lambda v, tg=tg, dn=dn: v.tensor_tensor(
                            ACC(tg)[:, dn * 512:(dn + 1) * 512],
                            ACC(tg)[:, dn * 512:(dn + 1) * 512],
                            TMP2[tg % 2], ALU.add), sem="sV")
                        tmp_free[tg % 2] = p.val("sV")
                    bank_free[bk] = ("sV", p.val("sV"))
            het_evac[0] = p.val("sV")
            out_evac_last[0] = p.val("sV")
        if e == 0:
            dump(p, "acc_sh_a", ACC_A.rearrange("p a b -> p (a b)"),
                 waits=[("sV", p.val("sV"))])
            dump(p, "acc_sh_b", ACC_B.rearrange("p a b -> p (a b)"),
                 waits=[("sV", p.val("sV"))])
        if e == 1:
            dump(p, "acc_e0_a", ACC_A.rearrange("p a b -> p (a b)"),
                 waits=[("sV", p.val("sV"))])
            dump(p, "acc_e0_b", ACC_B.rearrange("p a b -> p (a b)"),
                 waits=[("sV", p.val("sV"))])
    experts_done_v = p.val("sV")
    experts_done_t = p.val("sT")

    stats2 = LNStat(env)
    store_done = [0, 0]
    xold_free = [0, 0]
    uslot_free = [0, 0]
    for it in range(NT):
        s_old = XTa[:, it % 2, :]
        scr = XTa[:, 2 + it % 2, :]
        u = XTa[:, 4 + it % 2, :]
        stats2.n = (it % 2) * 112
        if it < 2:
            p.wait("scalar", "sT", experts_done_t)
            p.wait("scalar", "sV", experts_done_v)
            p.wait("vector", "sV", experts_done_v)
        p.wait("scalar", "sV", xold_free[it % 2])
        p.op("scalar", lambda g, it=it, s_old=s_old: g.dma_start(
            out=s_old, in_=xin[it * 128:(it + 1) * 128, :]), sem="dX", inc=16)
        xo = p.val("dX")
        p.wait("scalar", "sV", uslot_free[it % 2])
        lnv = emit_ln_tile(p, ACC(it), u, scr, stats2, EPS_LN)
        p.wait("vector", "dX", xo)
        p.wait("vector", "sV", lnv)
        p.wait("vector", "dS", store_done[it % 2])
        p.op("vector", lambda v, it=it, s_old=s_old, u=u: v.tensor_tensor(
            ACC(it), s_old, u, ALU.add), sem="sV")
        nv = p.val("sV")
        xold_free[it % 2] = nv
        uslot_free[it % 2] = nv
        p.wait("scalar", "sV", nv)
        p.op("scalar", lambda g, it=it: g.dma_start(
            out=xout[it * 128:(it + 1) * 128, :], in_=ACC(it)), sem="dS", inc=16)
        store_done[it % 2] = p.val("dS")


# version 13
# speedup vs baseline: 1.0730x; 1.0730x over previous
import contextlib
import numpy as np
import ml_dtypes
import concourse.bass as bass
import concourse.mybir as mybir
from concourse.bass_utils import run_bass_kernel_spmd

F32 = mybir.dt.float32
BF16 = mybir.dt.bfloat16
AF = mybir.ActivationFunctionType
ALU = mybir.AluOpType
AX = mybir.AxisListType

T = 2048
D = 1024
NT = T // 128
ND = D // 128
EPS_LN = 1e-5
EPS_RMS = 1e-6
NEG_BIG = -1.0e9

ENGINES = ("tensor", "vector", "scalar", "gpsimd", "sync")

HEAP_BYTES = 194 * 1024
SMALL_F32 = 3072


class Prog:
    def __init__(self, env, ename, eng):
        self.env = env
        self.e = ename
        self.eng = eng
        self.c = {k: 0 for k in env["sems"]}

    def op(self, ename, f, sem=None, inc=1):
        tgt = None
        if sem is not None:
            self.c[sem] += inc
            tgt = self.c[sem]
        if self.e == ename:
            ins = f(self.eng)
            if sem is not None:
                ins.then_inc(self.env["sems"][sem], inc)
        return tgt

    def wait(self, ename, sem, val):
        if self.e == ename and val is not None and val > 0:
            self.eng.wait_ge(self.env["sems"][sem], val)

    def val(self, sem):
        return self.c[sem]


def carve(heap, off_bytes, dtype, shape):
    assert off_bytes % 4 == 0
    n = 1
    for s in shape:
        n *= s
    if dtype == F32:
        v = heap[:, off_bytes // 4: off_bytes // 4 + n]
    else:
        assert n % 2 == 0
        v = heap[:, off_bytes // 4: off_bytes // 4 + n // 2].bitcast(BF16)
    if len(shape) == 1:
        return v
    pat = "p (" + " ".join(f"a{i}" for i in range(len(shape))) + ") -> p " + \
          " ".join(f"a{i}" for i in range(len(shape)))
    kw = {f"a{i}": s for i, s in enumerate(shape)}
    return v.rearrange(pat, **kw)


def emit_consts(p):
    env = p.env
    sm = env["small"]
    ident = env["ident"]
    mask = env["mask"]
    ones_r = env["ones_row"]
    ones_rb = env["ones_row_bf"]
    gv = p.op("gpsimd", lambda g: g.iota(
        mask, pattern=[[1, 128]], base=0, channel_multiplier=-1,
        allow_small_or_imprecise_dtypes=True), sem="sG")
    p.wait("gpsimd", "sG", gv)
    gv = p.op("gpsimd", lambda g: g.tensor_single_scalar(ident, mask, 0.0, ALU.is_equal), sem="sG")
    p.wait("gpsimd", "sG", gv)
    gv = p.op("gpsimd", lambda g: g.tensor_single_scalar(mask, mask, 0.0, ALU.is_gt), sem="sG")
    p.wait("gpsimd", "sG", gv)
    p.op("gpsimd", lambda g: g.tensor_scalar_mul(mask, mask, NEG_BIG), sem="sG")
    p.op("gpsimd", lambda g: g.memset(ones_r, 1.0), sem="sG")
    p.op("gpsimd", lambda g: g.memset(ones_rb, 1.0), sem="sG")
    p.op("gpsimd", lambda g: g.memset(env["ones_col_bf"], 1.0), sem="sG")
    scr_a = env["stats"][:, 0:16]
    scr_b = env["stats"][:, 16:32]
    gv = p.op("gpsimd", lambda g: g.iota(
        scr_a, pattern=[[1, 16]], base=0, channel_multiplier=0,
        allow_small_or_imprecise_dtypes=True), sem="sG")
    p.wait("gpsimd", "sG", gv)
    gv = p.op("gpsimd", lambda g: g.iota(
        scr_b, pattern=[[0, 16]], base=0, channel_multiplier=1,
        allow_small_or_imprecise_dtypes=True), sem="sG")
    p.wait("gpsimd", "sG", gv)
    gv = p.op("gpsimd", lambda g: g.tensor_single_scalar(
        scr_b, scr_b, 63.5, ALU.is_gt), sem="sG")
    p.wait("gpsimd", "sG", gv)
    for c in range(2):
        gv = p.op("gpsimd", lambda g, c=c: g.tensor_tensor(
            env["stats"][:, 32:48], scr_a, scr_b, ALU.subtract), sem="sG")
        p.wait("gpsimd", "sG", gv)
        gv = p.op("gpsimd", lambda g, c=c: g.tensor_single_scalar(
            env["sel64"][c], env["stats"][:, 32:48], 2.0 * c, ALU.is_equal), sem="sG")
        p.wait("gpsimd", "sG", gv)
    for m0 in range(4):
        gv = p.op("gpsimd", lambda g, m0=m0: g.tensor_single_scalar(
            env["self4"][m0], scr_a, float(m0), ALU.is_equal), sem="sG")
        p.wait("gpsimd", "sG", gv)
    gv = p.op("gpsimd", lambda g: g.tensor_single_scalar(
        env["sall4"], scr_a, 3.5, ALU.is_lt), sem="sG")
    p.wait("gpsimd", "sG", gv)


def emit_bias_pp(p, rows_sb, nchunk, out_pp, wait=None, scale=None, psum=None):
    env = p.env
    ps = psum if psum is not None else env["psB0"]
    if wait is not None:
        p.wait("tensor", *wait)
    ident = env["ident"]
    p.wait("tensor", "sG", env["consts_done"])
    p.wait("tensor", "sV", p.val("sV"))
    p.op("tensor", lambda t: t.transpose(ps[:, 0:nchunk], rows_sb, ident[0:nchunk, 0:nchunk]), sem="sT")
    tv = p.val("sT")
    p.wait("vector", "sT", tv)
    if scale is None:
        p.op("vector", lambda v: v.tensor_copy(out_pp, ps[:, 0:nchunk]), sem="sV")
    else:
        p.op("vector", lambda v: v.tensor_scalar_mul(out_pp, ps[:, 0:nchunk], scale), sem="sV")
    return p.val("sV")


def emit_bcast(p, row_sb, n, out_bc, wait=None):
    env = p.env
    ones_r = env["ones_row"]
    if wait is not None:
        p.wait("tensor", *wait)
    p.wait("tensor", "sG", env["consts_done"])
    p.wait("tensor", "sV", p.val("sV"))
    done = []
    for o in range(0, n, 512):
        w = min(512, n - o)
        ps = env["psB1"] if (o // 512) % 2 else env["psB0"]
        p.op("tensor", lambda t, o=o, w=w, ps=ps: t.matmul(
            ps[:, 0:w], ones_r, row_sb[:, o:o + w], start=True, stop=True), sem="sT")
        tv = p.val("sT")
        p.wait("vector", "sT", tv)
        p.op("vector", lambda v, o=o, w=w, ps=ps: v.tensor_copy(out_bc[:, o:o + w], ps[:, 0:w]), sem="sV")
        done.append(p.val("sV"))
    return p.val("sV")


class LNStat:
    def __init__(self, env):
        self.ap = env["stats"]
        self.n = 0

    def col(self):
        c = self.n
        self.n += 2
        assert self.n <= 256
        return self.ap[:, c:c + 1]

    def misc(self, idx):
        assert 0 <= idx < 32
        return self.ap[:, 224 + idx:225 + idx]


def emit_ln_tile(p, x_ap, out_ap, scr_ap, stats, eps, gbc=None, out2_ap=None):
    s1 = stats.col()
    s2 = stats.col()
    nm = stats.col()
    var = stats.col()
    var2 = stats.col()
    sd = stats.col()
    rst = stats.col()
    sv1 = None
    if gbc is None:
        sv1 = p.op("vector", lambda v: v.reduce_sum(s1, x_ap, axis=AX.X), sem="sV")
    sqv = p.op("scalar", lambda s: s.activation(
        scr_ap, x_ap, AF.Square, bias=0.0, scale=1.0, accum_out=s2), sem="sS")
    p.wait("vector", "sS", sqv)
    if gbc is None:
        p.wait("vector", "sV", sv1)
        nmv = p.op("vector", lambda v: v.tensor_scalar_mul(nm, s1, -1.0 / D), sem="sV")
        p.wait("vector", "sV", nmv)
        varv = p.op("vector", lambda v: v.tensor_tensor(var, nm, nm, ALU.mult), sem="sV")
        p.wait("vector", "sV", varv)
        v2 = p.op("vector", lambda v: v.scalar_tensor_tensor(
            var2, s2, 1.0 / D, var, ALU.mult, ALU.subtract), sem="sV")
        p.wait("vector", "sV", v2)
        p.op("vector", lambda v: v.tensor_scalar_add(var2, var2, float(eps)), sem="sV")
    else:
        p.op("vector", lambda v: v.tensor_scalar(
            var2, s2, 1.0 / D, float(eps), ALU.mult, ALU.add), sem="sV")
    vv = p.val("sV")
    p.wait("scalar", "sV", vv)
    p.op("scalar", lambda s: s.activation(sd, var2, AF.Sqrt, bias=0.0, scale=1.0), sem="sS")
    sv = p.val("sS")
    p.wait("vector", "sS", sv)
    rcv = p.op("vector", lambda v: v.reciprocal(rst, sd), sem="sV")
    p.wait("vector", "sV", rcv)
    if gbc is None:
        p.op("vector", lambda v: v.tensor_scalar(
            out_ap, x_ap, nm, rst, ALU.add, ALU.mult), sem="sV")
        if out2_ap is not None:
            p.op("vector", lambda v: v.tensor_scalar(
                out2_ap, x_ap, nm, rst, ALU.add, ALU.mult), sem="sV")
    else:
        p.op("vector", lambda v: v.scalar_tensor_tensor(
            out_ap, x_ap, rst, gbc, ALU.mult, ALU.mult), sem="sV")
        if out2_ap is not None:
            p.op("vector", lambda v: v.scalar_tensor_tensor(
                out2_ap, x_ap, rst, gbc, ALU.mult, ALU.mult), sem="sV")
    return p.val("sV")




def emit_bound_prep(p, env, qt, kt, sels_q, sels_k, PF, BROW, BT, NB, st,
                    qk_ready, barrier, nheads=4):
    psA = env["psA"]
    kmc = st.col()
    km16 = kmc[0:16, :]
    p.wait("scalar", "sS", barrier[0])
    p.wait("scalar", "sT", barrier[2])
    p.wait("scalar", "sV", qk_ready)
    p.wait("vector", "sT", barrier[2])
    p.wait("gpsimd", "sT", barrier[2])
    p.wait("tensor", "sS", barrier[0])
    bz = p.op("gpsimd", lambda g: g.memset(BROW[0:16, 0:T], 0.0), sem="sG")
    last_mm = [0]

    def side(src_t, psrow, sels):
        nch = len(sels)
        for c in range(nch):
            p.wait("scalar", "sT", last_mm[0])
            sqd = p.op("scalar", lambda s, c=c, src_t=src_t: s.activation(
                PF[:, 0:T], src_t[:, c, :], AF.Square, bias=0.0, scale=1.0), sem="sS")
            p.wait("tensor", "sS", sqd)
            for sl in range(4):
                p.op("tensor", lambda t, sl=sl, c=c: t.matmul(
                    psrow[0:16, sl * 512:(sl + 1) * 512],
                    sels[c], PF[:, sl * 512:(sl + 1) * 512],
                    start=(c == 0), stop=(c == nch - 1)),
                    sem="sT" if sl == 3 and c == nch - 1 else None)
            last_mm[0] = p.val("sT")
        return last_mm[0]

    kv = side(kt, psA, sels_k)
    p.wait("vector", "sT", kv)
    kmv = p.op("vector", lambda v: v.reduce_max(
        km16, psA[0:16, 0:T], axis=AX.X), sem="sV")
    p.wait("tensor", "sV", kmv)
    qv = side(qt, psA, sels_q)
    p.wait("scalar", "sT", qv)
    p.wait("scalar", "sV", kmv)
    p.wait("scalar", "sG", bz)
    sqv = p.op("scalar", lambda s: s.activation(
        BROW[0:16, 0:T], psA[0:16, 0:T], AF.Sqrt, bias=0.0, scale=km16), sem="sS")
    p.wait("sync", "sS", sqv)
    p.op("sync", lambda s: s.dma_start(out=BT, in_=BROW[0:16, 0:T],
                                       transpose=True), sem="dT", inc=16)
    btv = p.val("dT")
    p.wait("vector", "dT", btv)
    p.op("vector", lambda v: v.tensor_scalar_mul(
        NB, BT[:, :, 0:nheads], -1.0), sem="sV")
    return p.val("sV"), p.val("sS"), p.val("sT")


def emit_attn_core(p, cfg):
    env = p.env
    NPd = cfg["np"]
    P_ = cfg["P"]
    st = cfg["astat"]
    psA, psA2 = env["psA"], env["psA2"]
    dn0 = [st.col() for _ in range(NPd)]
    dn1 = [st.col() for _ in range(NPd)]
    dn2 = [st.col() for _ in range(NPd)]
    den = [st.col() for _ in range(NPd)]
    rcc = [st.col() for _ in range(NPd)]
    exp_done = [0, 0]
    avevac_done = [0, 0]
    avbank_free = [0, 0]
    trPA = [0] * NPd
    trPX = [0] * NPd
    avmm_done = [0] * NPd
    den_done = [0] * NPd
    den_used = [0] * NPd
    rcp_done = [0] * NPd
    pq = []

    def flush(pend):
        itp, ibp, ipp, gate = pend
        tb = psA2 if ibp == 0 else psA
        p.wait("tensor", "sS", gate)
        p.wait("tensor", "sV", avbank_free[(ibp + 1) % 2])
        mv = cfg["av"](tb, itp, ipp, trPA[ipp], trPX[ipp])
        avmm_done[ipp] = mv
        p.wait("vector", "sT", mv)
        p.wait("vector", "sV", rcp_done[ipp])
        ev = cfg["evac"](tb, itp, rcc[ipp])
        avevac_done[(ibp + 1) % 2] = ev
        avbank_free[(ibp + 1) % 2] = ev
        den_used[ipp] = ev

    for i, it in enumerate(cfg["iters"]):
        ib = i % 2
        ip = i % NPd
        buf = psA if ib == 0 else psA2
        nw = it["nw"]
        w1 = min(nw, 1024) - 512 if nw > 512 else 0
        wA = nw - 1024 if nw > 1024 else 0
        p.wait("tensor", "sS", exp_done[ib])
        p.wait("tensor", "sV", avevac_done[ib])
        if i < 2 and cfg.get("ready"):
            p.wait("tensor", "sV", cfg["ready"])
        if i < 2 and cfg.get("ready_s"):
            p.wait("tensor", "sS", cfg["ready_s"])
        sc1 = cfg["scores"](buf, it, 512, w1) if w1 else None
        scA = cfg["scores"](buf, it, 1024, wA) if wA else None
        scX = cfg["scores"](buf, it, 0, min(nw, 512))
        if len(pq) == 2:
            flush(pq.pop(0))
        pq.append([it, ib, ip, 0])
        mk = {}
        if cfg.get("mask"):
            part = it["diag_part"]
            p.wait("vector", "sT", {0: scX, 1: sc1, 2: scA}[part])
            mk[part] = cfg["mask"](buf, it)
        nb = cfg["bias"](it)
        p.wait("scalar", "sV", rcp_done[ip])
        p.wait("scalar", "sG", den_done[ip])
        p.wait("scalar", "dT", trPX[ip])
        eB1 = None
        if w1:
            p.wait("scalar", "sT", sc1)
            if 1 in mk:
                p.wait("scalar", "sV", mk[1])
            eB1 = p.op("scalar", lambda s: s.activation(
                P_[:, ip, 512:512 + w1], buf[:, 512:512 + w1], AF.Exp, bias=nb,
                scale=1.0, accum_out=dn1[ip]), sem="sS")
        eA = None
        if wA:
            p.wait("scalar", "sT", scA)
            if 2 in mk:
                p.wait("scalar", "sV", mk[2])
            eA = p.op("scalar", lambda s: s.activation(
                P_[:, ip, 1024:nw], buf[:, 1024:nw], AF.Exp, bias=nb,
                scale=1.0, accum_out=dn2[ip]), sem="sS")
        if w1:
            p.wait("sync", "sS", eA if eA is not None else eB1)
            p.wait("sync", "sT", avmm_done[ip])
            p.op("sync", lambda s: s.dma_start(
                out=cfg["PT"][:, ip, 4:nw // 128, :], in_=P_[:, ip, 512:nw],
                transpose=True), sem="dT", inc=16)
            trPA[ip] = p.val("dT")
        p.wait("scalar", "sT", scX)
        if 0 in mk:
            p.wait("scalar", "sV", mk[0])
        eX = p.op("scalar", lambda s: s.activation(
            P_[:, ip, 0:min(nw, 512)], buf[:, 0:min(nw, 512)], AF.Exp,
            bias=nb, scale=1.0, accum_out=dn0[ip]), sem="sS")
        p.wait("sync", "sS", eX)
        if not w1:
            p.wait("sync", "sT", avmm_done[ip])
        p.op("sync", lambda s: s.dma_start(
            out=cfg["PT"][:, ip, 0:min(nw, 512) // 128, :], in_=P_[:, ip, 0:min(nw, 512)],
            transpose=True), sem="dT", inc=16)
        trPX[ip] = p.val("dT")
        if not w1:
            trPA[ip] = trPX[ip]
        exp_done[ib] = eX
        pq[-1][3] = eB1 if eB1 is not None else eX
        p.wait("gpsimd", "sS", eX)
        p.wait("gpsimd", "sV", den_used[ip])
        dv = p.op("gpsimd", lambda g: (
            g.tensor_tensor(den[ip], dn0[ip], dn1[ip], ALU.add) if w1 else
            g.tensor_copy(den[ip], dn0[ip])), sem="sG")
        if wA:
            p.wait("gpsimd", "sG", dv)
            dv = p.op("gpsimd", lambda g: g.tensor_tensor(
                den[ip], den[ip], dn2[ip], ALU.add), sem="sG")
        den_done[ip] = dv
        p.wait("vector", "sG", dv)
        rcp_done[ip] = p.op("vector", lambda v: v.reciprocal(rcc[ip], den[ip]), sem="sV")
    for pend in pq:
        flush(pend)


def ceil_div(a, b):
    return (a + b - 1) // b


class StopEmission(Exception):
    pass


def dump(p, name, src_ap, waits=()):
    env = p.env
    d = env.get("dbg") or {}
    if name in d:
        for w in waits:
            p.wait("scalar", *w)
        p.op("scalar", lambda s: s.dma_start(out=d[name], in_=src_ap), sem="dS", inc=16)
    if env.get("stop") == name:
        raise StopEmission()


def emit_l0_attn(p, w, xin, xout, wpre):
    env = p.env
    heap = env["heap"]
    small = env["small"]
    FM = carve(heap, 0, BF16, [ND, T])
    QTG = carve(heap, 32768, BF16, [2, T])
    KTG = carve(heap, 40960, BF16, [2, T])
    VG = carve(heap, 49152, BF16, [NT, 256])
    LAT = carve(heap, 57344, BF16, [2, T])
    O = carve(heap, 65536, BF16, [NT, D])
    P = carve(heap, 98304, BF16, [3, T])
    PT = carve(heap, 110592, BF16, [3, NT, 128])
    XTa = carve(heap, 122880, F32, [6, D])
    LNB = carve(heap, 147456, BF16, [2, D])
    WQ = carve(heap, 151552, BF16, [ND, D])
    WL = carve(heap, 167936, BF16, [ND, 256])
    WK = carve(heap, 172032, BF16, [2, D])
    WV = carve(heap, 176128, BF16, [2, D])
    WO = carve(heap, 180224, BF16, [ND, D])

    BC1 = small[:, 0:1024]
    BC2 = small[:, 1024:2048]
    BPP = small[:, 2048:2080]
    stats = LNStat(env)

    psA = env["psA"]
    psA2 = env["psA2"]
    psB0 = env["psB0"]
    psB1 = env["psB1"]
    psC0 = env["psC0"]
    psC1 = env["psC1"]

    SCALE = 0.125

    rows_bq = XTa[0:8, 0, 0:128]
    rows_bl = XTa[0:2, 1, 0:128]
    rows_bk = XTa[0:8, 2, 0:128]
    row_bv = XTa[0:1, 3, :]
    row_bo = XTa[0:1, 4, :]
    p.op("scalar", lambda g: g.dma_start(
        out=rows_bq, in_=w["bq"].rearrange("(c p) -> c p", p=128)), sem="dX", inc=16)
    p.op("scalar", lambda g: g.dma_start(
        out=rows_bl, in_=w["bl"].rearrange("(c p) -> c p", p=128)), sem="dX", inc=16)
    p.op("scalar", lambda g: g.dma_start(
        out=rows_bk, in_=w["bk"].rearrange("(c p) -> c p", p=128)), sem="dX", inc=16)
    p.op("scalar", lambda g: g.dma_start(out=row_bv, in_=w["bv"].unsqueeze(0)), sem="dX", inc=16)
    p.op("scalar", lambda g: g.dma_start(out=row_bo, in_=w["bo"].unsqueeze(0)), sem="dX", inc=16)
    wload = p.val("dX")

    emit_bias_pp(p, rows_bq, 8, BPP[:, 0:8], wait=("dX", wload), psum=psB0)
    emit_bias_pp(p, rows_bl, 2, BPP[:, 8:10], psum=psB1)
    emit_bias_pp(p, rows_bk, 8, BPP[:, 10:18], psum=psC0)
    emit_bcast(p, row_bv, D, BC1)
    emit_bcast(p, row_bo, D, BC2)
    bias_done = p.val("sV")

    XS = carve(heap, 32768, F32, [NT, D])
    xs_v = [0] * 4
    for c_ in range(4):
        p.op("scalar", lambda g, c_=c_: g.dma_start(
            out=XS[:, c_ * 4:(c_ + 1) * 4, :],
            in_=xin[c_ * 512:(c_ + 1) * 512, :].rearrange("(a p) d -> p a d", p=128)),
            sem="dX", inc=16)
        xs_v[c_] = p.val("dX")
    ln_done = [0] * NT
    tr_done = [0] * NT
    lnb_free = [0, 0]
    for it in range(NT):
        ds = XS[:, it, :]
        scr = XTa[:, 2 + it % 2, :]
        if it < 2:
            p.wait("scalar", "sV", bias_done)
        if it % 4 == 0:
            p.wait("vector", "dX", xs_v[it // 4])
            p.wait("scalar", "dX", xs_v[it // 4])
        p.wait("vector", "dT", lnb_free[it % 2])
        ln_done[it] = emit_ln_tile(p, ds, LNB[:, it % 2, :], scr, stats, EPS_LN)
        p.wait("sync", "sV", ln_done[it])
        p.op("sync", lambda s, it=it: s.dma_start(
            out=FM[:, :, it * 128:(it + 1) * 128], in_=LNB[:, it % 2, :],
            transpose=True), sem="dT", inc=16)
        tr_done[it] = p.val("dT")
        lnb_free[it % 2] = tr_done[it]
    fm_ready = p.val("dT")
    dump(p, "xt", XTa.rearrange("p a b -> p (a b)"), waits=[("sV", p.val("sV"))])
    dump(p, "lnb", LNB.rearrange("p a b -> p (a b)"), waits=[("sV", p.val("sV"))])
    dump(p, "stl", p.env["stats"], waits=[("sV", p.val("sV"))])
    dump(p, "fm", FM.rearrange("p a b -> p (a b)"), waits=[("dT", fm_ready)])

    bank_free = {"psB0": 0, "psB1": 0, "psC0": 0, "psC1": 0}
    banks = {"psB0": psB0, "psB1": psB1, "psC0": psC0, "psC1": psC1}

    def proj_fm(dst, dst_dc, wsb, w_cols, kcs, bias_col, scale, n_ic=4, src=None):
        src_fm = FM if src is None else src
        for ic in range(n_ic):
            bk = "psB0" if ic % 2 == 0 else "psB1"
            ps = banks[bk]
            p.wait("tensor", "sV", bank_free[bk])
            for j, kc in enumerate(kcs):
                p.op("tensor", lambda t, ps=ps, kc=kc, ic=ic, j=j: t.matmul(
                    ps, wsb[:, kc, w_cols], src_fm[:, kc, ic * 512:(ic + 1) * 512],
                    start=(j == 0), stop=(j == len(kcs) - 1)),
                    sem="sT" if j == len(kcs) - 1 else None)
            mv = p.val("sT")
            p.wait("vector", "sT", mv)
            if bias_col is not None:
                p.op("vector", lambda v, ps=ps, ic=ic: v.tensor_scalar(
                    dst[:, dst_dc, ic * 512:(ic + 1) * 512], ps, bias_col, scale,
                    ALU.add, ALU.mult), sem="sV")
            elif scale != 1.0:
                p.op("vector", lambda v, ps=ps, ic=ic: v.tensor_scalar_mul(
                    dst[:, dst_dc, ic * 512:(ic + 1) * 512], ps, scale), sem="sV")
            else:
                p.op("vector", lambda v, ps=ps, ic=ic: v.tensor_copy(
                    dst[:, dst_dc, ic * 512:(ic + 1) * 512], ps), sem="sV")
            bank_free[bk] = p.val("sV")

    p.wait("tensor", "dT", fm_ready)
    p.wait("tensor", "dW", wpre["Wo"])
    p.wait("tensor", "dX", wload)
    p.wait("scalar", "sV", bias_done)
    for dc2 in range(2):
        proj_fm(LAT, dc2, WL, slice(dc2 * 128, (dc2 + 1) * 128), range(8),
                BPP[:, 8 + dc2:9 + dc2], 1.0)

    astat = LNStat(env)
    last_proj_mm = [0]
    attn_barrier = [0, 0, 0]

    for g in range(4):
        p.wait("tensor", "sS", attn_barrier[0])
        p.wait("tensor", "sV", attn_barrier[1])
        for dc2 in range(2):
            proj_fm(QTG, dc2, WQ, slice(g * 256 + dc2 * 128, g * 256 + (dc2 + 1) * 128),
                    range(8), BPP[:, g * 2 + dc2:g * 2 + dc2 + 1], SCALE)
        for dc2 in range(2):
            proj_fm(KTG, dc2, WK, slice(g * 256 + dc2 * 128, g * 256 + (dc2 + 1) * 128),
                    range(2), BPP[:, 10 + g * 2 + dc2:11 + g * 2 + dc2], 1.0, src=LAT)
        astat.n = 0
        PF = carve(heap, 98304, BF16, [3 * T])
        BROW = carve(heap, 110592, BF16, [3 * NT * 128])[0:16, 0:T]
        BTv = carve(heap, 106496, BF16, [16, 16])
        NB = env["stats"][:, 128:192].rearrange("p (a b) -> p a b", a=16, b=4)
        prep_v, prep_s, prep_t = emit_bound_prep(
            p, env, QTG, KTG, env["sel64"], env["sel64"], PF, BROW, BTv, NB,
            astat, p.val("sV"), attn_barrier)
        for jt in range(NT):
            bk = "psC0" if jt % 2 == 0 else "psC1"
            ps = banks[bk]
            p.wait("tensor", "sV", bank_free[bk])
            for kc in range(2):
                p.op("tensor", lambda t, ps=ps, kc=kc, jt=jt: t.matmul(
                    ps[:, 0:256], LAT[:, kc, jt * 128:(jt + 1) * 128],
                    WV[:, kc, g * 256:(g + 1) * 256],
                    start=(kc == 0), stop=(kc == 1)), sem="sT" if kc == 1 else None)
            mv = p.val("sT")
            p.wait("vector", "sT", mv)
            p.op("vector", lambda v, ps=ps, jt=jt: v.tensor_tensor(
                VG[:, jt, :], ps[:, 0:256], BC1[:, g * 256:(g + 1) * 256], ALU.add), sem="sV")
            bank_free[bk] = p.val("sV")
        last_proj_mm[0] = p.val("sT")
        vg_ready = p.val("sV")
        if g == 0:
            dump(p, "lat", LAT.rearrange("p a b -> p (a b)"), waits=[("sV", p.val("sV"))])
            dump(p, "qtg", QTG.rearrange("p a b -> p (a b)"), waits=[("sV", p.val("sV"))])
            dump(p, "ktg", KTG.rearrange("p a b -> p (a b)"), waits=[("sV", p.val("sV"))])
            dump(p, "vg", VG.rearrange("p a b -> p (a b)"), waits=[("sV", p.val("sV"))])

        iters = [{"h": h, "qi": qi, "nw": T, "head": g * 4 + h,
                  "hc": h // 2, "hp": (h % 2) * 64, "diag_half": -1}
                 for h in range(4) for qi in range(NT)]

        def sc_fn(buf, it, lo, wdt):
            jcs = list(range(lo // 512, (lo + wdt) // 512))
            for jc in jcs:
                p.op("tensor", lambda t, jc=jc, it=it, buf=buf: t.matmul(
                    buf[:, jc * 512:(jc + 1) * 512],
                    QTG[it["hp"]:it["hp"] + 64, it["hc"], it["qi"] * 128:(it["qi"] + 1) * 128],
                    KTG[it["hp"]:it["hp"] + 64, it["hc"], jc * 512:(jc + 1) * 512],
                    start=True, stop=True), sem="sT" if jc == jcs[-1] else None)
            return p.val("sT")

        def av_fn(bufp, itp, ipp, tAv, tXv):
            p.wait("tensor", "dT", tAv)
            for kc in range(4, NT):
                p.op("tensor", lambda t, kc=kc, ipp=ipp, itp=itp, bufp=bufp: t.matmul(
                    bufp[:, 512:576], PT[:, ipp, kc, :], VG[:, kc, itp["h"] * 64:itp["h"] * 64 + 64],
                    start=(kc == 4), stop=False))
            p.wait("tensor", "dT", tXv)
            for kc in range(4):
                p.op("tensor", lambda t, kc=kc, ipp=ipp, itp=itp, bufp=bufp: t.matmul(
                    bufp[:, 512:576], PT[:, ipp, kc, :], VG[:, kc, itp["h"] * 64:itp["h"] * 64 + 64],
                    start=False, stop=(kc == 3)), sem="sT" if kc == 3 else None)
            return p.val("sT")

        def evac_fn(bufp, itp, rc):
            p.op("vector", lambda v, bufp=bufp, itp=itp, rc=rc: v.tensor_scalar_mul(
                O[:, itp["qi"], itp["head"] * 64:itp["head"] * 64 + 64],
                bufp[:, 512:576], rc), sem="sV")
            return p.val("sV")

        emit_attn_core(p, {"iters": iters, "np": 3, "P": P, "PT": PT, "astat": astat,
                           "scores": sc_fn, "av": av_fn, "evac": evac_fn,
                           "ready": prep_v, "ready_s": prep_s,
                           "bias": lambda it, NB=NB: NB[:, it["qi"], it["h"]:it["h"] + 1]})
        attn_barrier = [p.val("sS"), p.val("sV"), p.val("sT")]

    attn_all_done = p.val("sS")
    attn_all_v = p.val("sV")
    dump(p, "o", O.rearrange("p a b -> p (a b)"), waits=[("sV", attn_all_v)])

    p.wait("sync", "sS", attn_all_done)
    p.wait("sync", "sV", attn_all_v)
    p.wait("sync", "sT", last_proj_mm[0])
    ot_done = [0] * NT
    for it in range(NT):
        p.op("sync", lambda s, it=it: s.dma_start(
            out=FM[:, :, it * 128:(it + 1) * 128], in_=O[:, it, :],
            transpose=True), sem="dT", inc=16)
        ot_done[it] = p.val("dT")

    stats2 = LNStat(env)
    stats2.n = 0
    out_slot_free = [0, 0]
    store_done = [0, 0]
    xold_slot_free = [0, 0]
    for it in range(NT):
        s_out = XTa[:, 4 + it % 2, :]
        s_old = XTa[:, it % 2, :]
        scr = XTa[:, 2 + it % 2, :]
        p.wait("tensor", "dT", ot_done[it])
        for n in range(2):
            bk = "psB0" if n == 0 else "psB1"
            ps = banks[bk]
            p.wait("tensor", "sV", bank_free[bk])
            for kc in range(ND):
                p.op("tensor", lambda t, ps=ps, kc=kc, it=it, n=n: t.matmul(
                    ps, FM[:, kc, it * 128:(it + 1) * 128],
                    WO[:, kc, n * 512:(n + 1) * 512],
                    start=(kc == 0), stop=(kc == ND - 1)),
                    sem="sT" if kc == ND - 1 else None)
            mv = p.val("sT")
            p.wait("vector", "sT", mv)
            p.wait("vector", "sV", out_slot_free[it % 2])
            p.wait("vector", "dS", store_done[it % 2])
            p.op("vector", lambda v, ps=ps, n=n, s_out=s_out: v.tensor_tensor(
                s_out[:, n * 512:(n + 1) * 512], ps, BC2[:, n * 512:(n + 1) * 512],
                ALU.add), sem="sV")
            bank_free[bk] = p.val("sV")
        p.wait("scalar", "sV", xold_slot_free[it % 2])
        p.op("scalar", lambda g, it=it, s_old=s_old: g.dma_start(
            out=s_old, in_=xin[it * 128:(it + 1) * 128, :]), sem="dX", inc=16)
        xo = p.val("dX")
        u = scr
        if stats2.n > 240:
            stats2.n = 0
        p.wait("scalar", "sV", p.val("sV"))
        lnv = emit_ln_tile(p, s_out, u, scr, stats2, EPS_LN)
        p.wait("vector", "dX", xo)
        p.wait("vector", "sV", lnv)
        p.op("vector", lambda v, s_out=s_out, s_old=s_old, u=u: v.tensor_tensor(
            s_out, s_old, u, ALU.add), sem="sV")
        nv = p.val("sV")
        xold_slot_free[it % 2] = nv
        p.wait("scalar", "sV", nv)
        p.op("scalar", lambda g, it=it, s_out=s_out: g.dma_start(
            out=xout[it * 128:(it + 1) * 128, :], in_=s_out), sem="dS", inc=16)
        store_done[it % 2] = p.val("dS")
        out_slot_free[it % 2] = nv


BLOCK_WEIGHTS = {
    "l0_attn": ["L0_attn_Wq", "L0_attn_bq", "L0_attn_Wl", "L0_attn_bl",
                "L0_attn_Wk", "L0_attn_bk", "L0_attn_Wv", "L0_attn_bv",
                "L0_attn_Wo", "L0_attn_bo"],
    "l0_ffn": ["L0_ffn_g1", "L0_ffn_g2", "L0_ffn_Wq", "L0_ffn_Wkc", "L0_ffn_Wvc",
               "L0_ffn_Wo", "L0_ffn_Wr", "L0_ffn_expert_bias", "L0_ffn_sW1",
               "L0_ffn_sW2", "L0_ffn_sW3", "L0_ffn_eW1", "L0_ffn_eW2", "L0_ffn_eW3"],
    "l1_attn": ["L1_attn_Wq", "L1_attn_Wk", "L1_attn_Wv", "L1_attn_Wp", "L1_attn_bp"],
    "l1_ffn": ["L1_ffn_w1_w", "L1_ffn_w1_s", "L1_ffn_w2_w", "L1_ffn_w2_s",
               "L1_ffn_h1_w", "L1_ffn_h1_s", "L1_ffn_h2_w", "L1_ffn_h2_s",
               "L1_ffn_w3_w", "L1_ffn_w3_s"],
}

WEIGHT_SHAPES = {
    "L0_attn_Wq": [1024, 1024], "L0_attn_bq": [1024],
    "L0_attn_Wl": [1024, 256], "L0_attn_bl": [256],
    "L0_attn_Wk": [256, 1024], "L0_attn_bk": [1024],
    "L0_attn_Wv": [256, 1024], "L0_attn_bv": [1024],
    "L0_attn_Wo": [1024, 1024], "L0_attn_bo": [1024],
    "L0_ffn_g1": [1024], "L0_ffn_g2": [1024],
    "L0_ffn_Wq": [1024, 1024], "L0_ffn_Wkc": [256, 1024], "L0_ffn_Wvc": [256, 1024],
    "L0_ffn_Wo": [1024, 1024], "L0_ffn_Wr": [8, 1024], "L0_ffn_expert_bias": [8],
    "L0_ffn_sW1": [1024, 1024], "L0_ffn_sW2": [1024, 1024], "L0_ffn_sW3": [1024, 1024],
    "L0_ffn_eW1": [8, 1024, 1024], "L0_ffn_eW2": [8, 1024, 1024], "L0_ffn_eW3": [8, 1024, 1024],
    "L1_attn_Wq": [1024, 1024], "L1_attn_Wk": [1024, 1024], "L1_attn_Wv": [1024, 1024],
    "L1_attn_Wp": [1024, 1024], "L1_attn_bp": [1024],
    "L1_ffn_w1_w": [4096, 1024], "L1_ffn_w1_s": [4096],
    "L1_ffn_w2_w": [4096, 1024], "L1_ffn_w2_s": [4096],
    "L1_ffn_h1_w": [4096, 4096], "L1_ffn_h1_s": [4096],
    "L1_ffn_h2_w": [4096, 4096], "L1_ffn_h2_s": [4096],
    "L1_ffn_w3_w": [1024, 4096], "L1_ffn_w3_s": [1024],
}

BLOCK_SCRATCH = {
    "l0_attn": {},
    "l0_ffn": {
        "wq": ([1024, 1024], BF16), "wkc": ([256, 1024], BF16),
        "wvc": ([256, 1024], BF16), "wo": ([1024, 1024], BF16),
        "wr": ([16, 1024], BF16),
        "sw1": ([1024, 1024], BF16), "sw2": ([1024, 1024], BF16),
        "sw3": ([1024, 1024], BF16),
        "ew1": ([8, 1024, 1024], BF16), "ew2": ([8, 1024, 1024], BF16),
        "ew3": ([8, 1024, 1024], BF16),
        "Ad": ([T, D], F32), "Xid": ([T, D], F32),
    },
    "l1_attn": {
        "q": ([1024, 1024], BF16), "k": ([1024, 1024], BF16),
        "v": ([1024, 1024], BF16), "p": ([1024, 1024], BF16),
    },
    "l1_ffn": {
        "w1": ([4096, 1024], BF16), "w2": ([4096, 1024], BF16),
        "h1": ([4096, 4096], BF16), "h2": ([4096, 4096], BF16),
        "w3": ([1024, 4096], BF16),
    },
}


def _short(name):
    return name.split("_", 2)[2]


def emit_barrier(p):
    vals = {s: p.val(s) for s in p.env["sems"]}
    for e in ENGINES:
        for s, v in vals.items():
            p.wait(e, s, v)


def emit_prefetch(p, blocks, dram):
    wpre = {}
    for bi, b in enumerate(blocks):
        d = {}
        if b == "l0_attn":
            heap = p.env["heap"]
            WQ = carve(heap, 151552, BF16, [ND, D])
            WL = carve(heap, 167936, BF16, [ND, 256])
            WK = carve(heap, 172032, BF16, [2, D])
            WV = carve(heap, 176128, BF16, [2, D])
            WO = carve(heap, 180224, BF16, [ND, D])
            for dst, key in [(WQ, "Wq"), (WL, "Wl"), (WK, "Wk"), (WV, "Wv"), (WO, "Wo")]:
                d[key] = p.op("gpsimd", lambda g, dst=dst, key=key: g.dma_start(
                    out=dst, in_=dram[f"L0_attn_{key}"].rearrange("(c p) f -> p c f", p=128)),
                    sem="dW", inc=16)
        elif b == "l0_ffn":
            for nm_, key in [("wq", "Wq"), ("wkc", "Wkc"), ("wvc", "Wvc"), ("wo", "Wo")]:
                d[nm_] = p.op("gpsimd", lambda g, nm_=nm_, key=key, bi=bi: g.dma_start(
                    out=dram[f"s{bi}_{nm_}"], in_=dram[f"L0_ffn_{key}"]), sem="dW", inc=16)
            d["wr"] = p.op("gpsimd", lambda g, bi=bi: g.dma_start(
                out=dram[f"s{bi}_wr"][0:8, :], in_=dram["L0_ffn_Wr"]), sem="dW", inc=16)
            for nm_, key in [("sw1", "sW1"), ("sw2", "sW2"), ("sw3", "sW3"),
                             ("ew1", "eW1"), ("ew2", "eW2"), ("ew3", "eW3")]:
                d[nm_] = p.op("gpsimd", lambda g, nm_=nm_, key=key, bi=bi: g.dma_start(
                    out=dram[f"s{bi}_{nm_}"], in_=dram[f"L0_ffn_{key}"]), sem="dW", inc=16)
        elif b == "l1_attn":
            for nm_, key in [("q", "Wq"), ("k", "Wk"), ("v", "Wv"), ("p", "Wp")]:
                d[nm_] = p.op("gpsimd", lambda g, nm_=nm_, key=key, bi=bi: g.dma_start(
                    out=dram[f"s{bi}_{nm_}"], in_=dram[f"L1_attn_{key}"]), sem="dW", inc=16)
        elif b == "l1_ffn":
            for nm_, key in [("w1", "w1_w"), ("w2", "w2_w"), ("h1", "h1_w"),
                             ("h2", "h2_w"), ("w3", "w3_w")]:
                d[nm_] = p.op("gpsimd", lambda g, nm_=nm_, key=key, bi=bi: g.dma_start(
                    out=dram[f"s{bi}_{nm_}"], in_=dram[f"L1_ffn_{key}"]), sem="dW", inc=16)
        wpre[bi] = d
    return wpre


def emit_program(p, blocks, dram):
    env = p.env
    emit_consts(p)
    env["consts_done"] = p.val("sG")
    wpre = emit_prefetch(p, blocks, dram)
    n = len(blocks)
    for bi, b in enumerate(blocks):
        if bi > 0:
            emit_barrier(p)
        xin = dram["xin"] if bi == 0 else dram[f"xmid{bi - 1}"]
        xout = dram["xout"] if bi == n - 1 else dram[f"xmid{bi}"]
        w = {_short(k): dram[k] for k in BLOCK_WEIGHTS[b]}
        scr_d = {k: dram[f"s{bi}_{k}"] for k in BLOCK_SCRATCH[b]}
        try:
            if b == "l0_attn":
                emit_l0_attn(p, w, xin, xout, wpre[bi])
            elif b == "l0_ffn":
                emit_l0_ffn(p, w, scr_d, xin, xout, wpre[bi])
            elif b == "l1_attn":
                emit_l1_attn(p, w, scr_d, xin, xout, wpre[bi])
            elif b == "l1_ffn":
                emit_l1_ffn(p, w, scr_d, xin, xout, wpre[bi])
            else:
                raise ValueError(b)
        except StopEmission:
            break
    p.wait("gpsimd", "dS", p.val("dS"))


def build_nc(blocks, dbg=None, stop=None):
    nc = bass.Bass("TRN2", target_bir_lowering=False)
    dram = {}
    dram["xin"] = nc.dram_tensor("xin", [T, D], F32, kind="ExternalInput").ap()
    dram["xout"] = nc.dram_tensor("xout", [T, D], F32, kind="ExternalOutput").ap()
    for bi in range(len(blocks) - 1):
        dram[f"xmid{bi}"] = nc.dram_tensor(f"xmid{bi}", [T, D], F32, kind="Internal").ap()
    for bi, b in enumerate(blocks):
        for wname in BLOCK_WEIGHTS[b]:
            dram[wname] = nc.dram_tensor(
                wname, WEIGHT_SHAPES[wname], F32, kind="ExternalInput").ap()
        for sname, (shape, dt_) in BLOCK_SCRATCH[b].items():
            dram[f"s{bi}_{sname}"] = nc.dram_tensor(
                f"s{bi}_{sname}", shape, dt_, kind="Internal").ap()
    dbg_aps = {}
    for name, (shape, dt_) in (dbg or {}).items():
        dbg_aps[name] = nc.dram_tensor("dbg_" + name, shape, dt_, kind="ExternalOutput").ap()

    with contextlib.ExitStack() as stack:
        e = stack.enter_context
        heap_h = e(nc.sbuf_tensor("heap", [128, HEAP_BYTES // 4], F32))
        small_h = e(nc.sbuf_tensor("small", [128, SMALL_F32], F32))
        psA = e(nc.psum_tensor("psA", [128, 2048], F32))
        psA2 = e(nc.psum_tensor("psA2", [128, 2048], F32))
        sems = {}
        for s in ("dW", "dX", "dT", "dS", "sT", "sV", "sS", "sG"):
            sems[s] = e(nc.semaphore(s))
        blk = e(nc.Block())

        heap = heap_h[:, :]
        small = small_h[:, :]
        env = {
            "nc": nc,
            "sems": sems,
            "heap": heap,
            "small": small,
            "psA": psA[:, :],
            "psA2": psA2[:, :],
            "psB0": psA2[:, 0:512],
            "psB1": psA2[:, 512:1024],
            "psC0": psA2[:, 1024:1536],
            "psC1": psA2[:, 1536:2048],
            "stats": small[:, 2080:2336],
            "ident": small[:, 2336:2464],
            "mask": small[:, 2464:2592],
            "ones_row": small[0:1, 2592:2720],
            "ones_row_bf": small[0:1, 2720:2784].bitcast(BF16),
            "ones_col_bf": small[:, 2916:2917].bitcast(BF16)[:, 0:1],
            "sel64": [small[:, 2920 + 8 * c:2928 + 8 * c].bitcast(BF16) for c in range(2)],
            "self4": [small[:, 2940 + 8 * m:2948 + 8 * m].bitcast(BF16) for m in range(4)],
            "sall4": small[:, 2975:2983].bitcast(BF16),
            "consts_done": 0,
            "dbg": dbg_aps,
            "stop": stop,
        }

        def mk_body(ename):
            def body(eng):
                p = Prog(env, ename, eng)
                emit_program(p, blocks, dram)
            return body

        blk.tensor(mk_body("tensor"))
        blk.vector(mk_body("vector"))
        blk.scalar(mk_body("scalar"))
        blk.gpsimd(mk_body("gpsimd"))
        blk.sync(mk_body("sync"))
    return nc


_NC_CACHE = {}


def _get_nc(blocks):
    key = tuple(blocks)
    if key not in _NC_CACHE:
        _NC_CACHE[key] = build_nc(list(blocks))
    return _NC_CACHE[key]


ALL_BLOCKS = ["l0_attn", "l0_ffn", "l1_attn", "l1_ffn"]


FUSED = True


def kernel(**inputs):
    x = np.ascontiguousarray(np.asarray(inputs["L0_attn_x"], dtype=np.float32))
    cur = [x[b] for b in range(8)]
    groups = [ALL_BLOCKS] if FUSED else [[b] for b in ALL_BLOCKS]
    for grp in groups:
        nc = _get_nc(grp)
        wnames = [wn for b in grp for wn in BLOCK_WEIGHTS[b]]
        in_maps = []
        for c in range(8):
            m = {"xin": cur[c]}
            for wname in wnames:
                m[wname] = np.ascontiguousarray(np.asarray(inputs[wname], dtype=np.float32))
            in_maps.append(m)
        res = run_bass_kernel_spmd(nc, in_maps, core_ids=list(range(8)))
        cur = [np.asarray(r["xout"]) for r in res.results]
    return np.stack(cur).astype(np.float32)


def emit_l1_attn(p, w, scr_d, xin, xout, wpre):
    env = p.env
    heap = env["heap"]
    small = env["small"]
    FM = carve(heap, 0, BF16, [ND, T])
    QTG = carve(heap, 32768, BF16, [2, T])
    KTG = carve(heap, 40960, BF16, [2, T])
    VG = carve(heap, 49152, BF16, [NT, 256])
    WTQ = carve(heap, 57344, BF16, [ND, 256])
    WTK = carve(heap, 61440, BF16, [ND, 256])
    WTV = carve(heap, 65536, BF16, [ND, 256])
    O = carve(heap, 69632, BF16, [NT, D])
    P = carve(heap, 102400, BF16, [3, T])
    PT = carve(heap, 114688, BF16, [3, NT, 128])
    XTa = carve(heap, 126976, F32, [6, D])
    LNB = carve(heap, 151552, BF16, [2, D])
    WPT = carve(heap, 155648, BF16, [ND, D])
    BC2 = small[:, 1024:2048]
    mask = env["mask"]
    stats = LNStat(env)
    psA, psA2 = env["psA"], env["psA2"]
    banks = {"psB0": env["psB0"], "psB1": env["psB1"],
             "psC0": env["psC0"], "psC1": env["psC1"]}
    bank_free = {k: 0 for k in banks}
    SCALE = 0.125

    row_bp = XTa[0:1, 3, :]
    p.op("scalar", lambda g: g.dma_start(out=row_bp, in_=w["bp"].unsqueeze(0)), sem="dX", inc=16)
    wload = p.val("dX")
    emit_bcast(p, row_bp, D, BC2, wait=("dX", wload))
    bias_done = p.val("sV")

    XS = carve(heap, 32768, F32, [NT, D])
    xs_v = [0] * 4
    for c_ in range(4):
        p.op("scalar", lambda g, c_=c_: g.dma_start(
            out=XS[:, c_ * 4:(c_ + 1) * 4, :],
            in_=xin[c_ * 512:(c_ + 1) * 512, :].rearrange("(a p) d -> p a d", p=128)),
            sem="dX", inc=16)
        xs_v[c_] = p.val("dX")
    ln_done = [0] * NT
    tr_done = [0] * NT
    lnb_free = [0, 0]
    for it in range(NT):
        ds = XS[:, it, :]
        scr = XTa[:, 2 + it % 2, :]
        stats.n = (it % 2) * 112
        if it < 2:
            p.wait("scalar", "sV", bias_done)
        if it % 4 == 0:
            p.wait("vector", "dX", xs_v[it // 4])
            p.wait("scalar", "dX", xs_v[it // 4])
        p.wait("vector", "dT", lnb_free[it % 2])
        ln_done[it] = emit_ln_tile(p, ds, LNB[:, it % 2, :], scr, stats, EPS_LN)
        p.wait("sync", "sV", ln_done[it])
        p.op("sync", lambda s, it=it: s.dma_start(
            out=FM[:, :, it * 128:(it + 1) * 128], in_=LNB[:, it % 2, :],
            transpose=True), sem="dT", inc=16)
        tr_done[it] = p.val("dT")
        lnb_free[it % 2] = tr_done[it]
    fm_ready = p.val("dT")
    p.wait("sync", "dW", wpre["p"])
    p.op("sync", lambda s: s.dma_start(out=WPT, in_=scr_d["p"], transpose=True), sem="dT", inc=16)
    wpt_done = p.val("dT")

    def proj_fm(dst, dst_dc, wsb, w_cols, kcs, scale, src=None):
        src_fm = FM if src is None else src
        for ic in range(4):
            bk = "psB0" if ic % 2 == 0 else "psB1"
            ps = banks[bk]
            p.wait("tensor", "sV", bank_free[bk])
            for j, kc in enumerate(kcs):
                p.op("tensor", lambda t, ps=ps, kc=kc, ic=ic, j=j: t.matmul(
                    ps, wsb[:, kc, w_cols], src_fm[:, kc, ic * 512:(ic + 1) * 512],
                    start=(j == 0), stop=(j == len(kcs) - 1)),
                    sem="sT" if j == len(kcs) - 1 else None)
            mv = p.val("sT")
            p.wait("vector", "sT", mv)
            if scale != 1.0:
                p.op("vector", lambda v, ps=ps, ic=ic: v.tensor_scalar_mul(
                    dst[:, dst_dc, ic * 512:(ic + 1) * 512], ps, scale), sem="sV")
            else:
                p.op("vector", lambda v, ps=ps, ic=ic: v.tensor_copy(
                    dst[:, dst_dc, ic * 512:(ic + 1) * 512], ps), sem="sV")
            bank_free[bk] = p.val("sV")

    astat = LNStat(env)
    attn_barrier = [0, 0, 0]
    last_gw_done = [0]
    last_proj_mm = [0]

    for g in range(4):
        p.wait("sync", "sT", last_proj_mm[0])
        p.wait("sync", "dW", wpre["v"])
        for wt, key in [(WTQ, "q"), (WTK, "k"), (WTV, "v")]:
            p.op("sync", lambda s, wt=wt, key=key: s.dma_start(
                out=wt, in_=scr_d[key][g * 256:(g + 1) * 256, :], transpose=True),
                sem="dT", inc=16)
        gw_done = p.val("dT")
        last_gw_done[0] = gw_done
        p.wait("tensor", "dT", fm_ready)
        p.wait("tensor", "dT", gw_done)
        p.wait("tensor", "sS", attn_barrier[0])
        p.wait("tensor", "sV", attn_barrier[1])
        for dc2 in range(2):
            proj_fm(QTG, dc2, WTQ, slice(dc2 * 128, (dc2 + 1) * 128), range(8), SCALE)
        for dc2 in range(2):
            proj_fm(KTG, dc2, WTK, slice(dc2 * 128, (dc2 + 1) * 128), range(8), 1.0)
        astat.n = 0
        PF = carve(heap, 102400, BF16, [3 * T])
        BROW = carve(heap, 114688, BF16, [3 * NT * 128])[0:16, 0:T]
        BTv = carve(heap, 110592, BF16, [16, 16])
        NB = env["stats"][:, 128:192].rearrange("p (a b) -> p a b", a=16, b=4)
        prep_v, prep_s, prep_t = emit_bound_prep(
            p, env, QTG, KTG, env["sel64"], env["sel64"], PF, BROW, BTv, NB,
            astat, p.val("sV"), attn_barrier)
        for jt in range(NT):
            bk = "psC0" if jt % 2 == 0 else "psC1"
            ps = banks[bk]
            p.wait("tensor", "sV", bank_free[bk])
            for kc in range(ND):
                p.op("tensor", lambda t, ps=ps, kc=kc, jt=jt: t.matmul(
                    ps[:, 0:256], FM[:, kc, jt * 128:(jt + 1) * 128],
                    WTV[:, kc, 0:256],
                    start=(kc == 0), stop=(kc == ND - 1)), sem="sT" if kc == ND - 1 else None)
            mv = p.val("sT")
            p.wait("vector", "sT", mv)
            p.op("vector", lambda v, ps=ps, jt=jt: v.tensor_copy(
                VG[:, jt, :], ps[:, 0:256]), sem="sV")
            bank_free[bk] = p.val("sV")
        last_proj_mm[0] = p.val("sT")
        vg_ready = p.val("sV")

        iters = []
        for h in range(4):
            for qi in range(NT):
                nw = (qi + 1) * 128
                iters.append({"h": h, "qi": qi, "nw": nw, "head": g * 4 + h,
                              "hc": h // 2, "hp": (h % 2) * 64,
                              "diag_part": 0 if qi < 4 else (1 if qi < 8 else 2)})

        def sc_fn(buf, it, lo, wdt):
            jcs = []
            o = lo
            while o < lo + wdt:
                w_ = min(512, lo + wdt - o)
                jcs.append((o, w_))
                o += w_
            for o, w_ in jcs:
                p.op("tensor", lambda t, o=o, w_=w_, it=it, buf=buf: t.matmul(
                    buf[:, o:o + w_],
                    QTG[it["hp"]:it["hp"] + 64, it["hc"], it["qi"] * 128:(it["qi"] + 1) * 128],
                    KTG[it["hp"]:it["hp"] + 64, it["hc"], o:o + w_],
                    start=True, stop=True), sem="sT" if (o, w_) == jcs[-1] else None)
            return p.val("sT")

        def mask_fn(buf, it):
            off = it["qi"] * 128
            mk = p.op("vector", lambda v, buf=buf, off=off: v.tensor_tensor(
                buf[:, off:off + 128], buf[:, off:off + 128], mask, ALU.add), sem="sV")
            return mk

        def av_fn(bufp, itp, ipp, tAv, tXv):
            nj = itp["qi"] + 1
            if nj > 4:
                p.wait("tensor", "dT", tAv)
                for kc in range(4, nj):
                    p.op("tensor", lambda t, kc=kc, ipp=ipp, itp=itp, bufp=bufp: t.matmul(
                        bufp[:, 512:576], PT[:, ipp, kc, :], VG[:, kc, itp["h"] * 64:itp["h"] * 64 + 64],
                        start=(kc == 4), stop=False))
            p.wait("tensor", "dT", tXv)
            for kc in range(min(nj, 4)):
                p.op("tensor", lambda t, kc=kc, ipp=ipp, itp=itp, bufp=bufp, nj=nj: t.matmul(
                    bufp[:, 512:576], PT[:, ipp, kc, :], VG[:, kc, itp["h"] * 64:itp["h"] * 64 + 64],
                    start=(nj <= 4 and kc == 0), stop=(kc == min(nj, 4) - 1)),
                    sem="sT" if kc == min(nj, 4) - 1 else None)
            return p.val("sT")

        def evac_fn(bufp, itp, rc):
            p.op("vector", lambda v, bufp=bufp, itp=itp, rc=rc: v.tensor_scalar_mul(
                O[:, itp["qi"], itp["head"] * 64:itp["head"] * 64 + 64],
                bufp[:, 512:576], rc), sem="sV")
            return p.val("sV")

        emit_attn_core(p, {"iters": iters, "np": 3, "P": P, "PT": PT, "astat": astat,
                           "scores": sc_fn, "av": av_fn, "evac": evac_fn,
                           "ready": prep_v, "ready_s": prep_s, "mask": mask_fn,
                           "bias": lambda it, NB=NB: NB[:, it["qi"], it["h"]:it["h"] + 1]})
        attn_barrier = [p.val("sS"), p.val("sV"), p.val("sT")]

    attn_all_done = p.val("sS")
    attn_all_v = p.val("sV")

    p.wait("sync", "sS", attn_all_done)
    p.wait("sync", "sV", attn_all_v)
    p.wait("sync", "sT", last_proj_mm[0])
    ot_done = [0] * NT
    for it in range(NT):
        p.op("sync", lambda s, it=it: s.dma_start(
            out=FM[:, :, it * 128:(it + 1) * 128], in_=O[:, it, :],
            transpose=True), sem="dT", inc=16)
        ot_done[it] = p.val("dT")

    stats2 = LNStat(env)
    out_slot_free = [0, 0]
    store_done = [0, 0]
    xold_slot_free = [0, 0]
    for it in range(NT):
        s_out = XTa[:, 4 + it % 2, :]
        s_old = XTa[:, it % 2, :]
        scr = XTa[:, 2 + it % 2, :]
        stats2.n = (it % 2) * 112
        p.wait("tensor", "dT", ot_done[it])
        p.wait("tensor", "dT", wpt_done)
        for n in range(2):
            bk = "psB0" if n == 0 else "psB1"
            ps = banks[bk]
            p.wait("tensor", "sV", bank_free[bk])
            for kc in range(ND):
                p.op("tensor", lambda t, ps=ps, kc=kc, it=it, n=n: t.matmul(
                    ps, FM[:, kc, it * 128:(it + 1) * 128],
                    WPT[:, kc, n * 512:(n + 1) * 512],
                    start=(kc == 0), stop=(kc == ND - 1)),
                    sem="sT" if kc == ND - 1 else None)
            mv = p.val("sT")
            p.wait("vector", "sT", mv)
            p.wait("vector", "sV", out_slot_free[it % 2])
            p.wait("vector", "dS", store_done[it % 2])
            p.op("vector", lambda v, ps=ps, n=n, s_out=s_out: v.tensor_tensor(
                s_out[:, n * 512:(n + 1) * 512], ps, BC2[:, n * 512:(n + 1) * 512],
                ALU.add), sem="sV")
            bank_free[bk] = p.val("sV")
        p.wait("scalar", "sV", xold_slot_free[it % 2])
        p.op("scalar", lambda g, it=it, s_old=s_old: g.dma_start(
            out=s_old, in_=xin[it * 128:(it + 1) * 128, :]), sem="dX", inc=16)
        xo = p.val("dX")
        u = scr
        p.wait("scalar", "sV", p.val("sV"))
        lnv = emit_ln_tile(p, s_out, u, scr, stats2, EPS_LN)
        p.wait("vector", "dX", xo)
        p.wait("vector", "sV", lnv)
        p.op("vector", lambda v, s_out=s_out, s_old=s_old, u=u: v.tensor_tensor(
            s_out, s_old, u, ALU.add), sem="sV")
        nv = p.val("sV")
        xold_slot_free[it % 2] = nv
        p.wait("scalar", "sV", nv)
        p.op("scalar", lambda g, it=it, s_out=s_out: g.dma_start(
            out=xout[it * 128:(it + 1) * 128, :], in_=s_out), sem="dS", inc=16)
        store_done[it % 2] = p.val("dS")
        out_slot_free[it % 2] = nv


def emit_l1_ffn(p, w, scr_d, xin, xout, wpre):
    env = p.env
    heap = env["heap"]
    small = env["small"]
    FM = carve(heap, 0, BF16, [ND, T])
    Y1 = carve(heap, 32768, BF16, [32, 1024])
    H1 = carve(heap, 98304, BF16, [32, 1024])
    XH = carve(heap, 98304, F32, [8, 1024])
    XTe = carve(heap, 131072, F32, [6, D])
    WC = [carve(heap, 163840, BF16, [32, 128]),
          carve(heap, 172032, BF16, [32, 128])]
    W3C = carve(heap, 163840, BF16, [32, 256])
    YC1 = [carve(heap, 180224, BF16, [ND, 128]), carve(heap, 182272, BF16, [ND, 128])]
    YC2 = [carve(heap, 184320, BF16, [ND, 128]), carve(heap, 186368, BF16, [ND, 128])]
    TMP = [carve(heap, 188416, F32, [512]), carve(heap, 190464, F32, [512])]
    LNB = carve(heap, 192512, BF16, [2, D])
    XTa = carve(heap, 32768, F32, [6, D])
    S3B = small[:, 0:1024]
    SPP = {"s1": small[:, 1024:1056], "s2": small[:, 1056:1088],
           "h1": small[:, 1088:1120], "h2": small[:, 1120:1152]}
    stats = LNStat(env)
    banks = {"psB0": env["psB0"], "psB1": env["psB1"],
             "psC0": env["psC0"], "psC1": env["psC1"]}
    bank_free = {k: ("sV", 0) for k in banks}

    rows = {}
    for i, key in enumerate(["w1_s", "w2_s", "h1_s", "h2_s"]):
        r = XTa[0:32, i, 0:128]
        rows[key] = r
        p.op("scalar", lambda g, r=r, key=key: g.dma_start(
            out=r, in_=w[key].rearrange("(c p) -> c p", p=128)), sem="dX", inc=16)
    row_s3 = XTa[0:1, 4, :]
    p.op("scalar", lambda g: g.dma_start(out=row_s3, in_=w["w3_s"].unsqueeze(0)), sem="dX", inc=16)
    wload = p.val("dX")
    emit_bias_pp(p, rows["w1_s"], 32, SPP["s1"], wait=("dX", wload), psum=env["psB0"])
    emit_bias_pp(p, rows["w2_s"], 32, SPP["s2"], psum=env["psB1"])
    emit_bias_pp(p, rows["h1_s"], 32, SPP["h1"], psum=env["psC0"])
    emit_bias_pp(p, rows["h2_s"], 32, SPP["h2"], psum=env["psC1"])
    emit_bcast(p, row_s3, D, S3B)
    pre_done = p.val("sV")

    XS = carve(heap, 98304, F32, [NT, D])
    xs_v = [0] * 4
    for c_ in range(4):
        p.op("scalar", lambda g, c_=c_: g.dma_start(
            out=XS[:, c_ * 4:(c_ + 1) * 4, :],
            in_=xin[c_ * 512:(c_ + 1) * 512, :].rearrange("(a p) d -> p a d", p=128)),
            sem="dX", inc=16)
        xs_v[c_] = p.val("dX")
    ln_done = [0] * NT
    lnb_free = [0, 0]
    for it in range(NT):
        ds = XS[:, it, :]
        scr = XTa[:, 2 + it % 2, :]
        stats.n = (it % 2) * 112
        if it < 2:
            p.wait("scalar", "sV", pre_done)
        if it % 4 == 0:
            p.wait("vector", "dX", xs_v[it // 4])
            p.wait("scalar", "dX", xs_v[it // 4])
        p.wait("vector", "dT", lnb_free[it % 2])
        ln_done[it] = emit_ln_tile(p, ds, LNB[:, it % 2, :], scr, stats, EPS_LN)
        p.wait("sync", "sV", ln_done[it])
        p.op("sync", lambda s, it=it: s.dma_start(
            out=FM[:, :, it * 128:(it + 1) * 128], in_=LNB[:, it % 2, :],
            transpose=True), sem="dT", inc=16)
        lnb_free[it % 2] = p.val("dT")
    fm_ready = p.val("dT")

    stats2 = LNStat(env)
    store_done = [0, 0]
    xold_free = [0, 0]
    uslot_free = [0, 0]
    ep_count = [0]
    tmp_free = [0, 0]

    for H in range(2):
        tok0 = H * 1024
        g0 = p.val("sT")
        yc_free = [g0, g0]
        for fo in range(32):
            b = fo % 2
            if fo < 2:
                p.wait("sync", "dW", wpre["w2"])
            p.wait("sync", "sT", yc_free[b])
            p.op("sync", lambda s, b=b, fo=fo: s.dma_start(
                out=YC1[b], in_=scr_d["w1"][fo * 128:(fo + 1) * 128, :],
                transpose=True), sem="dT", inc=16)
            p.op("sync", lambda s, b=b, fo=fo: s.dma_start(
                out=YC2[b], in_=scr_d["w2"][fo * 128:(fo + 1) * 128, :],
                transpose=True), sem="dT", inc=16)
            ycv = p.val("dT")
            if fo < 2:
                p.wait("tensor", "dT", fm_ready)
            p.wait("tensor", "dT", ycv)
            for ic in range(2):
                bka = "psB0" if ic % 2 == 0 else "psB1"
                bkb = "psC0" if ic % 2 == 0 else "psC1"
                psa_, psb_ = banks[bka], banks[bkb]
                p.wait("tensor", *bank_free[bka])
                for kc in range(ND):
                    p.op("tensor", lambda t, psa_=psa_, kc=kc, ic=ic, b=b: t.matmul(
                        psa_, YC1[b][:, kc, :], FM[:, kc, tok0 + ic * 512:tok0 + (ic + 1) * 512],
                        start=(kc == 0), stop=(kc == ND - 1)),
                        sem="sT" if kc == ND - 1 else None)
                amm = p.val("sT")
                p.wait("tensor", *bank_free[bkb])
                for kc in range(ND):
                    p.op("tensor", lambda t, psb_=psb_, kc=kc, ic=ic, b=b: t.matmul(
                        psb_, YC2[b][:, kc, :], FM[:, kc, tok0 + ic * 512:tok0 + (ic + 1) * 512],
                        start=(kc == 0), stop=(kc == ND - 1)),
                        sem="sT" if kc == ND - 1 else None)
                bmm = p.val("sT")
                p.wait("scalar", "sT", amm)
                p.wait("scalar", "sV", tmp_free[ic])
                p.op("scalar", lambda s, psa_=psa_, ic=ic, fo=fo: s.activation(
                    TMP[ic], psa_, AF.Silu, bias=0.0, scale=SPP["s1"][:, fo:fo + 1]), sem="sS")
                sil = p.val("sS")
                bank_free[bka] = ("sS", sil)
                p.wait("vector", "sT", bmm)
                p.wait("vector", "sS", sil)
                p.op("vector", lambda v, psb_=psb_, ic=ic, fo=fo: v.scalar_tensor_tensor(
                    Y1[:, fo, ic * 512:(ic + 1) * 512], psb_, SPP["s2"][:, fo:fo + 1],
                    TMP[ic], ALU.mult, ALU.mult), sem="sV")
                bank_free[bkb] = ("sV", p.val("sV"))
                tmp_free[ic] = p.val("sV")
            yc_free[b] = p.val("sT")
        y_done_v = p.val("sV")

        def h_phase(src, dst, wkey, spp, prev_done):
            g0h = p.val("sT")
            wc_free = [g0h, g0h]
            first = [True]
            for fo in range(32):
                b = fo % 2
                if fo < 2:
                    p.wait("sync", "dW", wpre[wkey])
                p.wait("sync", "sT", wc_free[b])
                p.op("sync", lambda s, b=b, fo=fo: s.dma_start(
                    out=WC[b], in_=scr_d[wkey][fo * 128:(fo + 1) * 128, :],
                    transpose=True), sem="dT", inc=16)
                wcv = p.val("dT")
                p.wait("tensor", "dT", wcv)
                for ic in range(2):
                    bk = "psB0" if ic % 2 == 0 else "psB1"
                    ps = banks[bk]
                    p.wait("tensor", *bank_free[bk])
                    if first[0]:
                        if prev_done:
                            p.wait("tensor", "sV", prev_done)
                        first[0] = False
                    for kc in range(32):
                        p.op("tensor", lambda t, ps=ps, kc=kc, ic=ic, b=b: t.matmul(
                            ps, WC[b][:, kc, :], src[:, kc, ic * 512:(ic + 1) * 512],
                            start=(kc == 0), stop=(kc == 31)),
                            sem="sT" if kc == 31 else None)
                    mv = p.val("sT")
                    p.wait("scalar", "sT", mv)
                    p.op("scalar", lambda s, ps=ps, ic=ic, fo=fo: s.activation(
                        dst[:, fo, ic * 512:(ic + 1) * 512], ps, AF.Gelu,
                        bias=0.0, scale=spp[:, fo:fo + 1]), sem="sS")
                    bank_free[bk] = ("sS", p.val("sS"))
                wc_free[b] = p.val("sT")
            return p.val("sS")

        h1_done = h_phase(Y1, H1, "h1", SPP["h1"], y_done_v)
        p.wait("tensor", "sS", h1_done)
        h2_done = h_phase(H1, Y1, "h2", SPP["h2"], 0)

        p.wait("sync", "sS", h2_done)
        p.wait("sync", "dW", wpre["w3"])
        xh_ready = [0] * 8
        for wc in range(4):
            p.op("sync", lambda s, wc=wc: s.dma_start(
                out=W3C, in_=scr_d["w3"][wc * 256:(wc + 1) * 256, :],
                transpose=True), sem="dT", inc=16)
            w3v = p.val("dT")
            p.wait("tensor", "dT", w3v)
            last_mm_this_wc = 0
            for sub in range(8):
                bk = ("psB0", "psB1", "psC0", "psC1")[sub % 4]
                ps = banks[bk]
                p.wait("tensor", *bank_free[bk])
                if wc == 0 and sub < 2:
                    p.wait("tensor", "sS", h2_done)
                for kc in range(32):
                    p.op("tensor", lambda t, ps=ps, kc=kc, sub=sub: t.matmul(
                        ps[:, 0:256], Y1[:, kc, sub * 128:(sub + 1) * 128],
                        W3C[:, kc, 0:256],
                        start=(kc == 0), stop=(kc == 31)),
                        sem="sT" if kc == 31 else None)
                mv = p.val("sT")
                last_mm_this_wc = mv
                p.wait("vector", "sT", mv)
                if wc == 0:
                    p.wait("vector", "dS", store_done[0] if sub % 2 == 0 else store_done[1])
                p.op("vector", lambda v, ps=ps, sub=sub, wc=wc: v.tensor_tensor(
                    XH[:, sub, wc * 256:(wc + 1) * 256], ps[:, 0:256],
                    S3B[:, wc * 256:(wc + 1) * 256], ALU.mult), sem="sV")
                bank_free[bk] = ("sV", p.val("sV"))
                if wc == 3:
                    xh_ready[sub] = p.val("sV")
            p.wait("sync", "sT", last_mm_this_wc)

        for sub in range(8):
            it = H * 8 + sub
            ep = ep_count[0]
            ep_count[0] += 1
            s_out = XH[:, sub, :]
            s_old = XTe[:, ep % 2, :]
            scr = XTe[:, 2 + ep % 2, :]
            u = XTe[:, 4 + ep % 2, :]
            stats2.n = (ep % 2) * 112
            if sub < 2:
                p.wait("scalar", "sV", xh_ready[sub])
            p.wait("scalar", "sV", xold_free[ep % 2])
            p.op("scalar", lambda g, it=it, s_old=s_old: g.dma_start(
                out=s_old, in_=xin[it * 128:(it + 1) * 128, :]), sem="dX", inc=16)
            xo = p.val("dX")
            p.wait("scalar", "sV", xh_ready[sub])
            p.wait("vector", "sV", xh_ready[sub])
            p.wait("scalar", "sV", uslot_free[ep % 2])
            lnv = emit_ln_tile(p, s_out, u, scr, stats2, EPS_LN)
            p.wait("vector", "dX", xo)
            p.wait("vector", "sV", lnv)
            p.wait("vector", "dS", store_done[ep % 2])
            p.op("vector", lambda v, s_out=s_out, s_old=s_old, u=u: v.tensor_tensor(
                s_out, s_old, u, ALU.add), sem="sV")
            nv = p.val("sV")
            xold_free[ep % 2] = nv
            uslot_free[ep % 2] = nv
            p.wait("scalar", "sV", nv)
            p.op("scalar", lambda g, it=it, s_out=s_out: g.dma_start(
                out=xout[it * 128:(it + 1) * 128, :], in_=s_out), sem="dS", inc=16)
            store_done[ep % 2] = p.val("dS")


def emit_l0_ffn(p, w, scr_d, xin, xout, wpre):
    env = p.env
    heap = env["heap"]
    small = env["small"]
    FM = carve(heap, 0, BF16, [ND, T])
    ACC_A = carve(heap, 0, F32, [ND, D])
    QT = carve(heap, 32768, BF16, [ND, T])
    NX2T = QT
    KCT = carve(heap, 65536, BF16, [2, T])
    VC = carve(heap, 73728, BF16, [NT, 256])
    EW1 = [carve(heap, 65536, BF16, [ND, D]),
           carve(heap, 114688, BF16, [ND, D])]
    O = carve(heap, 81920, BF16, [NT, D])
    ACC_B = carve(heap, 81920, F32, [ND, D])
    P = carve(heap, 114688, BF16, [2, T])
    PT = carve(heap, 122880, BF16, [2, NT, 128])
    XTa = carve(heap, 131072, F32, [6, D])
    EW2 = carve(heap, 131072, BF16, [ND, D])
    TMP2 = [carve(heap, 147456, F32, [512]), carve(heap, 149504, F32, [512])]
    LNB = carve(heap, 155648, BF16, [2, D])
    WQT = carve(heap, 159744, BF16, [ND, D])
    WOT = WQT
    EW3 = carve(heap, 159744, BF16, [ND, D])
    WKCT = carve(heap, 176128, BF16, [ND, 256])
    WVCT = carve(heap, 180224, BF16, [ND, 256])
    WRT = carve(heap, 184320, BF16, [ND, 16])
    HET = carve(heap, 184576, BF16, [ND, 512])
    RT = carve(heap, 192768, F32, [2, 5, 8])

    BC1 = small[:, 0:1024]
    BC2 = small[:, 1024:2048]
    EBB = small[:, 2048:2056]
    C = small[:, 2784:2912].rearrange("p (a b) -> p a b", a=16, b=8)
    stats = LNStat(env)
    psA, psA2 = env["psA"], env["psA2"]
    banks = {"psB0": env["psB0"], "psB1": env["psB1"],
             "psC0": env["psC0"], "psC1": env["psC1"]}
    bank_free = {k: ("sV", 0) for k in banks}

    def ACC(it):
        return (ACC_A if it < 8 else ACC_B)[:, it % 8, :]

    row_g1 = XTa[0:1, 3, :]
    row_g2 = XTa[0:1, 4, :]
    row_eb = XTa[0:1, 5, 0:8]
    p.op("scalar", lambda g: g.dma_start(out=row_g1, in_=w["g1"].unsqueeze(0)), sem="dX", inc=16)
    p.op("scalar", lambda g: g.dma_start(out=row_g2, in_=w["g2"].unsqueeze(0)), sem="dX", inc=16)
    p.op("scalar", lambda g: g.dma_start(out=row_eb, in_=w["expert_bias"].unsqueeze(0)), sem="dX", inc=16)
    wload = p.val("dX")
    emit_bcast(p, row_g1, D, BC1, wait=("dX", wload))
    emit_bcast(p, row_g2, D, BC2)
    emit_bcast(p, row_eb, 8, EBB)
    pre_done = p.val("sV")

    XS = carve(heap, 32768, F32, [NT, D])
    xs_v = [0] * 4
    for c_ in range(4):
        p.op("scalar", lambda g, c_=c_: g.dma_start(
            out=XS[:, c_ * 4:(c_ + 1) * 4, :],
            in_=xin[c_ * 512:(c_ + 1) * 512, :].rearrange("(a p) d -> p a d", p=128)),
            sem="dX", inc=16)
        xs_v[c_] = p.val("dX")
    ln_done = [0] * NT
    sa_free = [0, 0]
    lnb_free = [0, 0]
    ad_store = [0] * NT
    for it in range(NT):
        ds = XS[:, it, :]
        scr = XTa[:, 2 + it % 2, :]
        sA = XTa[:, 4 + it % 2, :]
        stats.n = (it % 2) * 112
        if it < 2:
            p.wait("scalar", "sV", pre_done)
        if it % 4 == 0:
            p.wait("vector", "dX", xs_v[it // 4])
            p.wait("scalar", "dX", xs_v[it // 4])
        p.wait("vector", "dS", sa_free[it % 2])
        lnv = emit_ln_tile(p, ds, sA, scr, stats, EPS_LN)
        p.wait("scalar", "sV", lnv)
        p.op("scalar", lambda g, it=it, sA=sA: g.dma_start(
            out=scr_d["Ad"][it * 128:(it + 1) * 128, :], in_=sA), sem="dS", inc=16)
        ad_store[it] = p.val("dS")
        p.wait("scalar", "sV", lnv)
        p.wait("vector", "dT", lnb_free[it % 2])
        rmv = emit_ln_tile(p, sA, LNB[:, it % 2, :], scr, stats, EPS_RMS, gbc=BC1)
        sa_free[it % 2] = ad_store[it]
        ln_done[it] = rmv
        p.wait("sync", "sV", rmv)
        p.op("sync", lambda s, it=it: s.dma_start(
            out=FM[:, :, it * 128:(it + 1) * 128], in_=LNB[:, it % 2, :],
            transpose=True), sem="dT", inc=16)
        lnb_free[it % 2] = p.val("dT")
    fm_ready = p.val("dT")

    p.wait("sync", "dW", wpre["wvc"])
    p.op("sync", lambda s: s.dma_start(out=WQT, in_=scr_d["wq"], transpose=True), sem="dT", inc=16)
    p.op("sync", lambda s: s.dma_start(out=WKCT, in_=scr_d["wkc"], transpose=True), sem="dT", inc=16)
    p.op("sync", lambda s: s.dma_start(out=WVCT, in_=scr_d["wvc"], transpose=True), sem="dT", inc=16)
    wt_done = p.val("dT")

    def proj_fm(dst, dst_dc, wsb, w_cols, kcs, scale, src=None):
        src_fm = FM if src is None else src
        for ic in range(4):
            bk = "psB0" if ic % 2 == 0 else "psB1"
            ps = banks[bk]
            p.wait("tensor", *bank_free[bk])
            for j, kc in enumerate(kcs):
                p.op("tensor", lambda t, ps=ps, kc=kc, ic=ic, j=j: t.matmul(
                    ps, wsb[:, kc, w_cols], src_fm[:, kc, ic * 512:(ic + 1) * 512],
                    start=(j == 0), stop=(j == len(kcs) - 1)),
                    sem="sT" if j == len(kcs) - 1 else None)
            mv = p.val("sT")
            p.wait("vector", "sT", mv)
            if scale != 1.0:
                p.op("vector", lambda v, ps=ps, ic=ic: v.tensor_scalar_mul(
                    dst[:, dst_dc, ic * 512:(ic + 1) * 512], ps, scale), sem="sV")
            else:
                p.op("vector", lambda v, ps=ps, ic=ic: v.tensor_copy(
                    dst[:, dst_dc, ic * 512:(ic + 1) * 512], ps), sem="sV")
            bank_free[bk] = ("sV", p.val("sV"))

    p.wait("tensor", "dT", fm_ready)
    p.wait("tensor", "dT", wt_done)
    for dc in range(ND):
        proj_fm(QT, dc, WQT, slice(dc * 128, (dc + 1) * 128), range(ND), 1.0 / 16.0)
    qt_proj_last = p.val("sT")
    for dc2 in range(2):
        proj_fm(KCT, dc2, WKCT, slice(dc2 * 128, (dc2 + 1) * 128), range(ND), 1.0)
    astat = LNStat(env)
    PF = carve(heap, 114688, BF16, [2 * T])
    BROW = carve(heap, 122880, BF16, [2 * NT * 128])[0:16, 0:T]
    BTv = carve(heap, 118784, BF16, [16, 16])
    NB = env["stats"][:, 128:192].rearrange("p (a b) -> p a b", a=16, b=4)
    sels_q = [env["self4"][c // 2] for c in range(8)]
    sels_k = [env["sall4"], env["sall4"]]
    prep_v, prep_s, prep_t = emit_bound_prep(
        p, env, QT, KCT, sels_q, sels_k, PF, BROW, BTv, NB,
        astat, p.val("sV"), (p.val("sS"), p.val("sV"), p.val("sT")))
    for jt in range(NT):
        bk = "psC0" if jt % 2 == 0 else "psC1"
        ps = banks[bk]
        p.wait("tensor", *bank_free[bk])
        for kc in range(ND):
            p.op("tensor", lambda t, ps=ps, kc=kc, jt=jt: t.matmul(
                ps[:, 0:256], FM[:, kc, jt * 128:(jt + 1) * 128],
                WVCT[:, kc, 0:256],
                start=(kc == 0), stop=(kc == ND - 1)), sem="sT" if kc == ND - 1 else None)
        mv = p.val("sT")
        p.wait("vector", "sT", mv)
        p.op("vector", lambda v, ps=ps, jt=jt: v.tensor_copy(VC[:, jt, :], ps[:, 0:256]), sem="sV")
        bank_free[bk] = ("sV", p.val("sV"))
    proj_last = p.val("sT")
    vc_ready = p.val("sV")

    iters = [{"h": h, "qi": qi, "nw": T, "diag_half": -1}
             for h in range(4) for qi in range(NT)]

    def sc_fn(buf, it, lo, wdt):
        jcs = list(range(lo // 512, (lo + wdt) // 512))
        for jc in jcs:
            for kc2 in range(2):
                p.op("tensor", lambda t, jc=jc, kc2=kc2, it=it, buf=buf: t.matmul(
                    buf[:, jc * 512:(jc + 1) * 512],
                    QT[:, 2 * it["h"] + kc2, it["qi"] * 128:(it["qi"] + 1) * 128],
                    KCT[:, kc2, jc * 512:(jc + 1) * 512],
                    start=(kc2 == 0), stop=(kc2 == 1)),
                    sem="sT" if jc == jcs[-1] and kc2 == 1 else None)
        return p.val("sT")

    def av_fn(bufp, itp, ipp, tAv, tXv):
        p.wait("tensor", "dT", tAv)
        for kc in range(4, NT):
            p.op("tensor", lambda t, kc=kc, ipp=ipp, bufp=bufp: t.matmul(
                bufp[:, 512:768], PT[:, ipp, kc, :], VC[:, kc, :],
                start=(kc == 4), stop=False))
        p.wait("tensor", "dT", tXv)
        for kc in range(4):
            p.op("tensor", lambda t, kc=kc, ipp=ipp, bufp=bufp: t.matmul(
                bufp[:, 512:768], PT[:, ipp, kc, :], VC[:, kc, :],
                start=False, stop=(kc == 3)), sem="sT" if kc == 3 else None)
        return p.val("sT")

    def evac_fn(bufp, itp, rc):
        p.op("vector", lambda v, bufp=bufp, itp=itp, rc=rc: v.tensor_scalar_mul(
            O[:, itp["qi"], itp["h"] * 256:(itp["h"] + 1) * 256],
            bufp[:, 512:768], rc), sem="sV")
        return p.val("sV")

    emit_attn_core(p, {"iters": iters, "np": 2, "P": P, "PT": PT, "astat": astat,
                       "scores": sc_fn, "av": av_fn, "evac": evac_fn,
                       "ready": prep_v, "ready_s": prep_s,
                       "bias": lambda it: NB[:, it["qi"], it["h"]:it["h"] + 1]})
    attn_done_s = p.val("sS")
    attn_done_t = p.val("sT")
    attn_done_v = p.val("sV")

    p.wait("sync", "sS", attn_done_s)
    p.wait("sync", "sV", attn_done_v)
    p.wait("sync", "sT", proj_last)
    ot_done = [0] * NT
    for it in range(NT):
        p.op("sync", lambda s, it=it: s.dma_start(
            out=FM[:, :, it * 128:(it + 1) * 128], in_=O[:, it, :],
            transpose=True), sem="dT", inc=16)
        ot_done[it] = p.val("dT")
    p.wait("sync", "sT", qt_proj_last)
    p.wait("sync", "dW", wpre["wr"])
    p.op("sync", lambda s: s.dma_start(out=WOT, in_=scr_d["wo"], transpose=True), sem="dT", inc=16)
    p.op("sync", lambda s: s.dma_start(out=WRT, in_=scr_d["wr"], transpose=True), sem="dT", inc=16)
    wot_done = p.val("dT")

    xi_store = [0] * NT
    nx2_tr = [0] * NT
    aslot_free = [0, 0]
    xislot_free = [0, 0]
    for it in range(NT):
        s_a = XTa[:, it % 2, :]
        scr = XTa[:, 2 + it % 2, :]
        s_xi = XTa[:, 4 + it % 2, :]
        stats.n = (it % 2) * 112
        p.wait("tensor", "dT", ot_done[it])
        p.wait("tensor", "dT", wot_done)
        p.wait("scalar", "dS", ad_store[it])
        p.wait("scalar", "sV", aslot_free[it % 2])
        p.op("scalar", lambda g, it=it, s_a=s_a: g.dma_start(
            out=s_a, in_=scr_d["Ad"][it * 128:(it + 1) * 128, :]), sem="dX", inc=16)
        av_ = p.val("dX")
        for n in range(2):
            bk = "psB0" if n == 0 else "psB1"
            ps = banks[bk]
            p.wait("tensor", *bank_free[bk])
            for kc in range(ND):
                p.op("tensor", lambda t, ps=ps, kc=kc, it=it, n=n: t.matmul(
                    ps, FM[:, kc, it * 128:(it + 1) * 128],
                    WOT[:, kc, n * 512:(n + 1) * 512],
                    start=(kc == 0), stop=(kc == ND - 1)),
                    sem="sT" if kc == ND - 1 else None)
            mv = p.val("sT")
            p.wait("vector", "sT", mv)
            p.wait("vector", "dX", av_)
            p.wait("vector", "dS", xislot_free[it % 2])
            p.op("vector", lambda v, ps=ps, n=n, s_xi=s_xi, s_a=s_a: v.tensor_tensor(
                s_xi[:, n * 512:(n + 1) * 512], ps, s_a[:, n * 512:(n + 1) * 512],
                ALU.add), sem="sV")
            bank_free[bk] = ("sV", p.val("sV"))
        xiv = p.val("sV")
        aslot_free[it % 2] = xiv
        p.wait("scalar", "sV", xiv)
        p.op("scalar", lambda g, it=it, s_xi=s_xi: g.dma_start(
            out=scr_d["Xid"][it * 128:(it + 1) * 128, :], in_=s_xi), sem="dS", inc=16)
        xi_store[it] = p.val("dS")
        xislot_free[it % 2] = xi_store[it]
        p.wait("scalar", "sV", xiv)
        p.wait("vector", "dT", lnb_free[it % 2])
        rmv = emit_ln_tile(p, s_xi, LNB[:, it % 2, :], scr, stats, EPS_RMS, gbc=BC2)
        p.wait("sync", "sV", rmv)
        if it < 2:
            p.wait("sync", "sT", attn_done_t)
        p.op("sync", lambda s, it=it: s.dma_start(
            out=NX2T[:, :, it * 128:(it + 1) * 128], in_=LNB[:, it % 2, :],
            transpose=True), sem="dT", inc=16)
        nx2_tr[it] = p.val("dT")
        lnb_free[it % 2] = nx2_tr[it]
    nx2_ready = p.val("dT")

    m1c = [stats.misc(12), stats.misc(14)]
    m2c = [stats.misc(16), stats.misc(18)]
    e1c = [stats.misc(20), stats.misc(22)]
    e2c = [stats.misc(24), stats.misc(26)]
    d12 = [stats.misc(28), stats.misc(30)]
    for it in range(NT):
        par = it % 2
        lg = RT[:, par, 0, :]
        rw = RT[:, par, 1, :]
        rw2 = RT[:, par, 2, :]
        eq1 = RT[:, par, 3, :]
        eq2 = RT[:, par, 4, :]
        bk = "psC0" if par == 0 else "psC1"
        ps = banks[bk]
        p.wait("tensor", "dT", nx2_tr[it])
        p.wait("tensor", "dT", wot_done)
        p.wait("tensor", *bank_free[bk])
        for kc in range(ND):
            p.op("tensor", lambda t, ps=ps, kc=kc, it=it: t.matmul(
                ps[:, 0:8], NX2T[:, kc, it * 128:(it + 1) * 128], WRT[:, kc, 0:8],
                start=(kc == 0), stop=(kc == ND - 1)), sem="sT" if kc == ND - 1 else None)
        mv = p.val("sT")
        p.wait("vector", "sT", mv)
        vv = p.op("vector", lambda v, ps=ps, lg=lg: v.tensor_tensor(lg, ps[:, 0:8], EBB, ALU.add), sem="sV")
        bank_free[bk] = ("sV", vv)
        p.wait("vector", "sV", vv)
        vv = p.op("vector", lambda v, lg=lg, par=par: v.reduce_max(
            m1c[par], lg, axis=AX.X, negate=True), sem="sV")
        p.wait("scalar", "sV", vv)
        sv_ = p.op("scalar", lambda s, lg=lg, rw=rw, par=par: s.activation(
            rw, lg, AF.Exp, bias=m1c[par], scale=1.0, accum_out=d12[par]), sem="sS")
        p.wait("vector", "sS", sv_)
        vv = p.op("vector", lambda v, par=par: v.reciprocal(m2c[par], d12[par]), sem="sV")
        p.wait("vector", "sV", vv)
        vv = p.op("vector", lambda v, rw=rw, par=par: v.tensor_scalar_mul(
            rw, rw, m2c[par]), sem="sV")
        p.wait("vector", "sV", vv)
        vv = p.op("vector", lambda v, rw=rw, par=par: v.reduce_max(
            m1c[par], rw, axis=AX.X), sem="sV")
        p.wait("vector", "sV", vv)
        vv = p.op("vector", lambda v, rw=rw, eq1=eq1, par=par: v.tensor_scalar(
            eq1, rw, m1c[par], None, ALU.is_ge), sem="sV")
        p.wait("vector", "sV", vv)
        vv = p.op("vector", lambda v, rw=rw, rw2=rw2, eq1=eq1: v.scalar_tensor_tensor(
            rw2, eq1, NEG_BIG, rw, ALU.mult, ALU.add), sem="sV")
        p.wait("vector", "sV", vv)
        vv = p.op("vector", lambda v, rw2=rw2, par=par: v.reduce_max(
            m2c[par], rw2, axis=AX.X), sem="sV")
        p.wait("vector", "sV", vv)
        vv = p.op("vector", lambda v, rw2=rw2, eq2=eq2, par=par: v.tensor_scalar(
            eq2, rw2, m2c[par], None, ALU.is_ge), sem="sV")
        p.wait("scalar", "sV", vv)
        sv_ = p.op("scalar", lambda s, par=par: s.activation(
            e1c[par], m1c[par], AF.Exp, bias=0.0, scale=1.0), sem="sS")
        p.wait("scalar", "sS", sv_)
        sv_ = p.op("scalar", lambda s, par=par: s.activation(
            e2c[par], m2c[par], AF.Exp, bias=0.0, scale=1.0), sem="sS")
        p.wait("vector", "sS", sv_)
        vv = p.op("vector", lambda v, par=par: v.tensor_tensor(
            d12[par], e1c[par], e2c[par], ALU.add), sem="sV")
        p.wait("vector", "sV", vv)
        vv = p.op("vector", lambda v, par=par: v.reciprocal(m1c[par], d12[par]), sem="sV")
        p.wait("vector", "sV", vv)
        vv = p.op("vector", lambda v, par=par: v.tensor_tensor(
            e1c[par], e1c[par], m1c[par], ALU.mult), sem="sV")
        p.wait("vector", "sV", vv)
        vv = p.op("vector", lambda v, par=par: v.tensor_tensor(
            e2c[par], e2c[par], m1c[par], ALU.mult), sem="sV")
        p.wait("vector", "sV", vv)
        vv = p.op("vector", lambda v, eq1=eq1, par=par: v.tensor_scalar(
            eq1, eq1, e1c[par], None, ALU.mult), sem="sV")
        p.wait("vector", "sV", vv)
        vv = p.op("vector", lambda v, eq1=eq1, eq2=eq2, it=it, par=par: v.scalar_tensor_tensor(
            C[:, it, :], eq2, e2c[par], eq1, ALU.mult, ALU.add), sem="sV")
    router_done = p.val("sV")
    dump(p, "c", small[:, 2784:2912], waits=[("sV", router_done)])

    acc_init = [0] * NT
    for it in range(NT):
        p.wait("scalar", "dS", xi_store[it])
        if it == 0:
            p.wait("scalar", "sT", p.val("sT"))
            p.wait("scalar", "dT", ot_done[NT - 1])
        p.op("scalar", lambda g, it=it: g.dma_start(
            out=ACC(it), in_=scr_d["Xid"][it * 128:(it + 1) * 128, :]), sem="dX", inc=16)
        acc_init[it] = p.val("dX")
    acc_ready = p.val("dX")

    h_mm_last = [0, 0]
    e3_mm_last = [0]
    e2_mm_last = [0]
    het_evac = [0]
    out_evac_last = [0]
    tmp_free = [0, 0]
    for e in range(9):
        b = e % 2
        if e == 0:
            w1src, w2src, w3src = scr_d["sw1"], scr_d["sw2"], scr_d["sw3"]
        else:
            w1src = scr_d["ew1"][e - 1]
            w2src = scr_d["ew2"][e - 1]
            w3src = scr_d["ew3"][e - 1]
        if e == 0:
            p.wait("sync", "sS", attn_done_s)
            p.wait("sync", "sT", p.val("sT"))
            p.wait("sync", "dX", acc_ready)
            p.wait("sync", "sV", router_done)
            p.wait("sync", "dW", wpre["sw3"])
        if e == 1:
            p.wait("sync", "dW", wpre["ew3"])
        p.wait("sync", "sT", h_mm_last[b])
        p.op("sync", lambda s, b=b, w1src=w1src: s.dma_start(
            out=EW1[b], in_=w1src, transpose=True), sem="dT", inc=16)
        p.wait("sync", "sT", e3_mm_last[0])
        p.op("sync", lambda s, w3src=w3src: s.dma_start(
            out=EW3, in_=w3src, transpose=True), sem="dT", inc=16)
        w13_done = p.val("dT")
        p.wait("sync", "sT", e2_mm_last[0])
        p.op("sync", lambda s, w2src=w2src: s.dma_start(
            out=EW2, in_=w2src, transpose=True), sem="dT", inc=16)
        w2_done = p.val("dT")
        for it4 in range(4):
            p.wait("tensor", "dT", w13_done)
            if e == 0 and it4 == 0:
                p.wait("tensor", "dT", nx2_ready)
            for fc in range(ND):
                bka = "psB0" if fc % 2 == 0 else "psB1"
                bkb = "psC0" if fc % 2 == 0 else "psC1"
                psa_, psb_ = banks[bka], banks[bkb]
                p.wait("tensor", *bank_free[bka])
                for kc in range(ND):
                    p.op("tensor", lambda t, psa_=psa_, kc=kc, fc=fc, b=b, it4=it4: t.matmul(
                        psa_, EW1[b][:, kc, fc * 128:(fc + 1) * 128],
                        NX2T[:, kc, it4 * 512:(it4 + 1) * 512],
                        start=(kc == 0), stop=(kc == ND - 1)),
                        sem="sT" if kc == ND - 1 else None)
                amm = p.val("sT")
                p.wait("tensor", *bank_free[bkb])
                for kc in range(ND):
                    p.op("tensor", lambda t, psb_=psb_, kc=kc, fc=fc, it4=it4: t.matmul(
                        psb_, EW3[:, kc, fc * 128:(fc + 1) * 128],
                        NX2T[:, kc, it4 * 512:(it4 + 1) * 512],
                        start=(kc == 0), stop=(kc == ND - 1)),
                        sem="sT" if kc == ND - 1 else None)
                bmm = p.val("sT")
                h_mm_last[b] = bmm
                e3_mm_last[0] = bmm
                p.wait("scalar", "sT", amm)
                p.wait("scalar", "sV", tmp_free[fc % 2])
                p.op("scalar", lambda s, psa_=psa_, fc=fc: s.activation(
                    TMP2[fc % 2], psa_, AF.Silu, bias=0.0, scale=1.0), sem="sS")
                sil = p.val("sS")
                bank_free[bka] = ("sS", sil)
                p.wait("vector", "sT", bmm)
                p.wait("vector", "sS", sil)
                p.op("vector", lambda v, psb_=psb_, fc=fc: v.scalar_tensor_tensor(
                    HET[:, fc, :], psb_, 1.0, TMP2[fc % 2], ALU.mult, ALU.mult), sem="sV")
                hv = p.val("sV")
                bank_free[bkb] = ("sV", hv)
                tmp_free[fc % 2] = hv
            h_done = p.val("sV")
            p.wait("tensor", "dT", w2_done)
            for dn in range(2):
                for sub in range(4):
                    tg = it4 * 4 + sub
                    bk = ("psB0", "psB1", "psC0", "psC1")[(dn * 4 + sub) % 4]
                    ps = banks[bk]
                    p.wait("tensor", *bank_free[bk])
                    if dn == 0 and sub == 0:
                        p.wait("tensor", "sV", h_done)
                    for fc in range(ND):
                        p.op("tensor", lambda t, ps=ps, fc=fc, sub=sub, dn=dn: t.matmul(
                            ps, HET[:, fc, sub * 128:(sub + 1) * 128],
                            EW2[:, fc, dn * 512:(dn + 1) * 512],
                            start=(fc == 0), stop=(fc == ND - 1)),
                            sem="sT" if fc == ND - 1 else None)
                    mv = p.val("sT")
                    e2_mm_last[0] = mv
                    p.wait("vector", "sT", mv)
                    if e == 0 and it4 == 0 and dn == 0 and sub < 2:
                        p.wait("vector", "dX", acc_ready)
                        p.wait("vector", "sV", router_done)
                    if e == 0:
                        p.op("vector", lambda v, ps=ps, tg=tg, dn=dn: v.tensor_tensor(
                            ACC(tg)[:, dn * 512:(dn + 1) * 512],
                            ACC(tg)[:, dn * 512:(dn + 1) * 512], ps, ALU.add), sem="sV")
                    else:
                        tv_ = p.op("vector", lambda v, ps=ps, tg=tg, e=e: v.tensor_scalar_mul(
                            TMP2[tg % 2], ps, C[:, tg, e - 1:e]), sem="sV")
                        p.wait("vector", "sV", tv_)
                        p.op("vector", lambda v, tg=tg, dn=dn: v.tensor_tensor(
                            ACC(tg)[:, dn * 512:(dn + 1) * 512],
                            ACC(tg)[:, dn * 512:(dn + 1) * 512],
                            TMP2[tg % 2], ALU.add), sem="sV")
                        tmp_free[tg % 2] = p.val("sV")
                    bank_free[bk] = ("sV", p.val("sV"))
            het_evac[0] = p.val("sV")
            out_evac_last[0] = p.val("sV")
        if e == 0:
            dump(p, "acc_sh_a", ACC_A.rearrange("p a b -> p (a b)"),
                 waits=[("sV", p.val("sV"))])
            dump(p, "acc_sh_b", ACC_B.rearrange("p a b -> p (a b)"),
                 waits=[("sV", p.val("sV"))])
        if e == 1:
            dump(p, "acc_e0_a", ACC_A.rearrange("p a b -> p (a b)"),
                 waits=[("sV", p.val("sV"))])
            dump(p, "acc_e0_b", ACC_B.rearrange("p a b -> p (a b)"),
                 waits=[("sV", p.val("sV"))])
    experts_done_v = p.val("sV")
    experts_done_t = p.val("sT")

    stats2 = LNStat(env)
    store_done = [0, 0]
    xold_free = [0, 0]
    uslot_free = [0, 0]
    for it in range(NT):
        s_old = XTa[:, it % 2, :]
        scr = XTa[:, 2 + it % 2, :]
        u = XTa[:, 4 + it % 2, :]
        stats2.n = (it % 2) * 112
        if it < 2:
            p.wait("scalar", "sT", experts_done_t)
            p.wait("scalar", "sV", experts_done_v)
            p.wait("vector", "sV", experts_done_v)
        p.wait("scalar", "sV", xold_free[it % 2])
        p.op("scalar", lambda g, it=it, s_old=s_old: g.dma_start(
            out=s_old, in_=xin[it * 128:(it + 1) * 128, :]), sem="dX", inc=16)
        xo = p.val("dX")
        p.wait("scalar", "sV", uslot_free[it % 2])
        lnv = emit_ln_tile(p, ACC(it), u, scr, stats2, EPS_LN)
        p.wait("vector", "dX", xo)
        p.wait("vector", "sV", lnv)
        p.wait("vector", "dS", store_done[it % 2])
        p.op("vector", lambda v, it=it, s_old=s_old, u=u: v.tensor_tensor(
            ACC(it), s_old, u, ALU.add), sem="sV")
        nv = p.val("sV")
        xold_free[it % 2] = nv
        uslot_free[it % 2] = nv
        p.wait("scalar", "sV", nv)
        p.op("scalar", lambda g, it=it: g.dma_start(
            out=xout[it * 128:(it + 1) * 128, :], in_=ACC(it)), sem="dS", inc=16)
        store_done[it % 2] = p.val("dS")
